# version 01
# baseline (speedup 1.0000x reference)
import os
import numpy as np
import ml_dtypes
from contextlib import ExitStack

KSUB = os.environ.get("KSUB", "all")

import concourse.bass as bass
import concourse.mybir as mybir
from concourse.bass_utils import run_bass_kernel_spmd

F32 = mybir.dt.float32
BF16 = mybir.dt.bfloat16
AF = mybir.ActivationFunctionType
ALU = mybir.AluOpType
AX = mybir.AxisListType.X
BF = ml_dtypes.bfloat16

P = 128
D = 1024
S = 2048
NPC = D // P
NH = 16
HD = 64
DC = 256
TT = 256
NTT = S // TT
QT = 256
NQT = S // QT
NKC = S // P
LN_EPS = 1e-5
NEXP = 8
DMID = 4096

STAGES = ("attn_a", "ffn1", "attn_b", "ffn2")

_UN = [0]


def _uq(nm):
    _UN[0] += 1
    return f"{nm}_{_UN[0]}"




ROLES = (
    "dw", "dx",
    "cast", "smm", "sq", "rowd", "rowa", "bc", "app", "prj", "pev", "pev2",
    "vtr", "vte",
    "sc", "ex", "cl", "avg", "on", "ob", "oev", "yp", "cons", "msk",
    "qe", "mx", "ch", "chp", "chv", "rp", "rp2", "gt", "mid", "ymm", "yev",
    "hdma", "aux",
)


class Kb:
    def __init__(self, nc, es):
        self.nc = nc
        self.es = es
        self.blk = es.enter_context(nc.Block())
        self.sems = {}
        self.cnt = {}
        self.waited = {}
        self.maxwait = {}
        self.trace = {e: [] for e in ("pe", "dve", "act", "pool", "sync")}
        self.cur = None
        for name in ROLES:
            self.sems[name] = es.enter_context(nc.semaphore("s_" + name))
            self.cnt[name] = 0

    def inc(self, role, instr, n=1):
        instr.then_inc(self.sems[role], n)
        self.cnt[role] += n
        if self.cur:
            self.trace[self.cur].append(("i", role, n))
        return (role, self.cnt[role])

    def dma(self, role, instr):
        return self.inc(role, instr, 16)

    def wait(self, eng, engname, *toks):
        for tok in toks:
            if tok is None:
                continue
            if isinstance(tok, (list, tuple)) and tok and isinstance(tok[0], (list, tuple)):
                self.wait(eng, engname, *tok)
                continue
            role, val = tok
            if val <= 0:
                continue
            key = (engname, role)
            if self.waited.get(key, -1) >= val:
                continue
            self.maxwait.setdefault(role, []).append((val, engname))
            if engname in self.trace:
                self.trace[engname].append(("w", role, val))
            eng.wait_ge(self.sems[role], val)
            self.waited[key] = val

    def check_waits(self):
        bad = []
        for role, lst in self.maxwait.items():
            mx = max(v for v, _ in lst)
            if mx > self.cnt[role]:
                culprits = sorted({(v, e) for v, e in lst if v > self.cnt[role]})
                bad.append((role, self.cnt[role], culprits[:5]))
        assert not bad, f"waits exceed final counts (hang): {bad}" 

    def tok(self, role):
        return (role, self.cnt[role])

    def _sec(self, name, attr, fn):
        prev = self.cur
        self.cur = name
        getattr(self.blk, attr)(fn)
        self.cur = prev

    def pe(self, fn):
        self._sec("pe", "tensor", fn)

    def dve(self, fn):
        self._sec("dve", "vector", fn)

    def act(self, fn):
        self._sec("act", "scalar", fn)

    def pool(self, fn):
        self._sec("pool", "gpsimd", fn)

    def sync(self, fn):
        self._sec("sync", "sync", fn)

    def check_deadlock(self):
        pos = {e: 0 for e in self.trace}
        sem = {r: 0 for r in self.cnt}
        progress = True
        while progress:
            progress = False
            for e, tr in self.trace.items():
                while pos[e] < len(tr):
                    kind, role, v = tr[pos[e]]
                    if kind == "i":
                        sem[role] += v
                        pos[e] += 1
                        progress = True
                    else:
                        if sem[role] >= v:
                            pos[e] += 1
                            progress = True
                        else:
                            break
        stuck = {e: (pos[e], len(tr), self.trace[e][pos[e]] if pos[e] < len(tr) else None)
                 for e, tr in self.trace.items() if pos[e] < len(tr)}
        assert not stuck, f"DEADLOCK: engines stuck at {stuck}; sems={ {k: v for k, v in sem.items() if any(s[2] and s[2][1] == k for s in stuck.values())} }" 


def load_weight(kb, g, sb, dram_ap, free_tok=None):
    K, M = dram_ap.shape[0], dram_ap.shape[1]
    if free_tok is not None:
        kb.wait(g, "pool", free_tok)
    nb = (K + P - 1) // P
    for b in range(nb):
        rows = min(P, K - b * P)
        ins = g.dma_start(out=sb[0:rows, b * M:(b + 1) * M],
                          in_=dram_ap[b * P:b * P + rows, :])
        kb.dma("dw", ins)
    return kb.tok("dw")



def _prep_shared(i):
    sh = {}
    f32 = np.float32
    sh["aa_dq"] = i["L0_attn_a_W_dq"]
    sh["aa_uq"] = i["L0_attn_a_W_uq"]
    sh["aa_dkv"] = i["L0_attn_a_W_dkv"]
    sh["aa_uk"] = i["L0_attn_a_W_uk"]
    sh["aa_uv"] = i["L0_attn_a_W_uv"]
    sh["aa_o"] = i["L0_attn_a_W_o"]
    sh["ab_dq"] = i["L0_attn_b_W_DQ"]
    sh["ab_uq"] = i["L0_attn_b_W_UQ"]
    sh["ab_dkv"] = i["L0_attn_b_W_DKV"]
    sh["ab_uk"] = i["L0_attn_b_W_UK"]
    sh["ab_uv"] = i["L0_attn_b_W_UV"]
    sh["ab_o"] = i["L0_attn_b_W_O"]
    qr = np.asarray(i["L0_attn_b_W_QR"]).reshape(DC, NH, 32)
    sh["ab_qr12"] = np.ascontiguousarray(
        np.concatenate([qr[:, :, 0::2], qr[:, :, 1::2]], axis=2).reshape(DC, NH * 32))
    sh["ab_qr21"] = np.ascontiguousarray(
        np.concatenate([qr[:, :, 1::2], qr[:, :, 0::2]], axis=2).reshape(DC, NH * 32))
    kr = np.asarray(i["L0_attn_b_W_KR"])
    sh["ab_kr12"] = np.ascontiguousarray(np.concatenate([kr[:, 0::2], kr[:, 1::2]], axis=1))
    sh["ab_kr21"] = np.ascontiguousarray(np.concatenate([kr[:, 1::2], kr[:, 0::2]], axis=1))
    sh["f_wq"] = i["L0_ffn_Wq"]
    sh["f_wk"] = i["L0_ffn_Wk"]
    sh["f_wv"] = i["L0_ffn_Wv"]
    sh["f_wp"] = i["L0_ffn_Wp"]
    sh["f_gw"] = i["L0_ffn_gate_W"]
    sh["f_w1"] = i["L0_ffn_exp_W1"]
    sh["f_w2"] = i["L0_ffn_exp_W2"]
    sh["f_b2"] = i["L0_ffn_exp_b2"]

    def feat_col(vec):
        return np.ascontiguousarray(np.asarray(vec, dtype=f32).reshape(NPC, P).T)

    sh["c_ln1g"] = feat_col(i["L0_ffn_ln1_g"])
    sh["c_ln1b"] = feat_col(i["L0_ffn_ln1_b"])
    sh["c_ln2g"] = feat_col(i["L0_ffn_ln2_g"])
    sh["c_ln2b"] = feat_col(i["L0_ffn_ln2_b"])
    sh["c_bq"] = feat_col(i["L0_ffn_bq"])
    sh["c_bk"] = feat_col(i["L0_ffn_bk"])
    sh["c_bv"] = feat_col(i["L0_ffn_bv"])
    sh["c_bp"] = feat_col(i["L0_ffn_bp"])
    sh["c_gateb"] = np.ascontiguousarray(
        np.tile(np.asarray(i["L0_ffn_gate_b"], dtype=f32)[None, :], (P, 1)))
    sh["c_b1"] = np.ascontiguousarray(
        np.asarray(i["L0_ffn_exp_b1"], dtype=f32).reshape(NEXP, 32, P).transpose(0, 2, 1))
    sh["c_identf"] = np.eye(P, dtype=f32)
    sh["c_identb"] = np.eye(P, dtype=f32).astype(BF)
    jj = np.arange(16, dtype=np.float64)
    inv = 10000.0 ** (-2.0 * jj / 32.0)
    ang = np.arange(S, dtype=np.float64)[:, None] * inv[None, :]
    cosr = np.cos(ang).T.astype(f32)
    sinr = np.sin(ang).T.astype(f32)
    cc = np.zeros((P, S), dtype=f32)
    ss = np.zeros((P, S), dtype=f32)
    for b in range(4):
        cc[32 * b:32 * b + 16] = cosr
        cc[32 * b + 16:32 * b + 32] = cosr
        ss[32 * b:32 * b + 16] = -sinr
        ss[32 * b + 16:32 * b + 32] = sinr
    sh["c_cc"] = cc
    sh["c_ss"] = ss
    sel = np.zeros((NEXP, NEXP * P), dtype=f32)
    for e in range(NEXP):
        sel[e, e * P:(e + 1) * P] = 1.0
    sh["c_sel"] = sel.astype(BF)
    m0 = (np.arange(QT)[None, :] >= np.arange(P)[:, None] + 0).astype(f32)
    m128 = (np.arange(QT)[None, :] >= np.arange(P)[:, None] + 128).astype(f32)
    sh["c_mask0"] = m0.astype(BF)
    sh["c_mask128"] = m128.astype(BF)
    return sh


def prep_inputs(inputs):
    sh = _prep_shared(inputs)
    x = np.asarray(inputs["L0_attn_a_x"])
    x_fm = np.ascontiguousarray(x.transpose(0, 2, 1))
    in_maps = []
    for c in range(8):
        m = dict(sh)
        m["x_fm"] = np.ascontiguousarray(x_fm[2 * c:2 * c + 2])
        in_maps.append(m)
    return in_maps



def build_core(stages=STAGES):
    nc = bass.Bass("TRN2", target_bir_lowering=False)
    es = ExitStack()
    dram = {}

    def din(name, shape, dtype=F32):
        dram[name] = nc.dram_tensor(name, list(shape), dtype, kind="ExternalInput").ap()

    din("x_fm", [2, D, S])
    for nm, shp in [
        ("aa_dq", [D, DC]), ("aa_uq", [DC, D]), ("aa_dkv", [D, DC]),
        ("aa_uk", [DC, D]), ("aa_uv", [DC, D]), ("aa_o", [D, D]),
        ("ab_dq", [D, DC]), ("ab_uq", [DC, D]), ("ab_dkv", [D, DC]),
        ("ab_uk", [DC, D]), ("ab_uv", [DC, D]), ("ab_o", [D, D]),
        ("ab_qr12", [DC, NH * 32]), ("ab_qr21", [DC, NH * 32]),
        ("ab_kr12", [D, 32]), ("ab_kr21", [D, 32]),
        ("f_wq", [D, D]), ("f_wk", [D, D]), ("f_wv", [D, D]), ("f_wp", [D, D]),
        ("f_gw", [D, NEXP]), ("f_w1", [NEXP, D, DMID]), ("f_w2", [NEXP, DMID, D]),
        ("f_b2", [NEXP, D]),
        ("c_ln1g", [P, NPC]), ("c_ln1b", [P, NPC]), ("c_ln2g", [P, NPC]),
        ("c_ln2b", [P, NPC]), ("c_bq", [P, NPC]), ("c_bk", [P, NPC]),
        ("c_bv", [P, NPC]), ("c_bp", [P, NPC]), ("c_gateb", [P, NEXP]),
        ("c_b1", [NEXP, P, 32]), ("c_identf", [P, P]),
        ("c_cc", [P, S]), ("c_ss", [P, S]),
    ]:
        din(nm, shp)
    din("c_identb", [P, P], BF16)
    din("c_sel", [NEXP, NEXP * P], BF16)
    din("c_mask0", [P, QT], BF16)
    din("c_mask128", [P, QT], BF16)

    y_hbm = nc.dram_tensor("y_fm", [2, D, S], F32, kind="ExternalOutput").ap()
    ctx_qr_dram = nc.dram_tensor("qr_scratch", [P, 4 * S], BF16, kind="Internal").ap()
    h2_dram = nc.dram_tensor("h2_scratch", [D, S], BF16,
                             kind="ExternalOutput" if os.environ.get("KDBG") else "Internal").ap()
    if os.environ.get("KDBG"):
        ctx_dbg = nc.dram_tensor("w8_dbg", [NEXP, S], BF16, kind="ExternalOutput").ap()
        ctx_dbg2 = nc.dram_tensor("rt_dbg", [P, 32], F32, kind="ExternalOutput").ap()
        ctx_dbg3 = nc.dram_tensor("rt_dbg2", [NEXP, P], F32, kind="ExternalOutput").ap()
    else:
        ctx_dbg = None

    kb = Kb(nc, es)

    def t_sbuf(name, shape, dtype):
        return es.enter_context(nc.sbuf_tensor(name, list(shape), dtype))

    ctx = dict(nc=nc, kb=kb, es=es, dram=dram, h2_dram=h2_dram,
               qr_dram=ctx_qr_dram, w8_dbg=ctx_dbg,
               rt_dbg=ctx_dbg2 if os.environ.get("KDBG") else None,
               rt_dbg2=ctx_dbg3 if os.environ.get("KDBG") else None,
               moe_dbg=nc.dram_tensor("moe_dbg", [P, 16384], F32,
                                      kind="ExternalOutput").ap()
               if os.environ.get("KDBG") else None)
    ctx["x_sb"] = t_sbuf("x_sb", [P, NPC * S], F32)
    ctx["ones_col"] = t_sbuf("ones_col", [P, 1], BF16)
    ctx["ones_row"] = t_sbuf("ones_row", [1, P], F32)
    ctx["ones_rowb"] = t_sbuf("ones_rowb", [1, P], BF16)
    ctx["eps1"] = t_sbuf("eps1", [1, 1], F32)
    ctx["identf"] = t_sbuf("identf", [P, P], F32)
    ctx["identb"] = t_sbuf("identb", [P, P], BF16)
    ctx["cst"] = t_sbuf("cst", [P, 72], F32)
    ctx["mask0"] = t_sbuf("mask0", [P, QT], BF16)
    ctx["mask128"] = t_sbuf("mask128", [P, QT], BF16)
    ctx["rows"] = t_sbuf("rows", [1, 4 * TT], F32)
    ctx["rows2"] = t_sbuf("rows2", [1, 4 * TT], F32)
    ctx["arf0"] = t_sbuf("arf0", [P, NPC * TT], F32)
    ctx["arf1"] = t_sbuf("arf1", [P, NPC * TT], F32)
    ctx["arb0"] = t_sbuf("arb0", [P, NPC * TT], BF16)
    ctx["arb1"] = t_sbuf("arb1", [P, NPC * TT], BF16)
    ctx["chead"] = t_sbuf("chead", [P, 16], F32)

    CST_OFF = dict(ln1g=0, ln1b=8, ln2g=16, ln2b=24, bq=32, bk=40, bv=48,
                   bp=56, gateb=64)
    ctx["cst_off"] = CST_OFF

    def _loads(q):
        for nm, off in [("c_ln1g", 0), ("c_ln1b", 8), ("c_ln2g", 16),
                        ("c_ln2b", 24), ("c_bq", 32), ("c_bk", 40),
                        ("c_bv", 48), ("c_bp", 56), ("c_gateb", 64)]:
            kb.dma("dx", q.dma_start(out=ctx["cst"][:, off:off + dram[nm].shape[1]],
                                     in_=dram[nm][:, :]))
        kb.dma("dx", q.dma_start(out=ctx["identf"][:], in_=dram["c_identf"][:, :]))
        kb.dma("dx", q.dma_start(out=ctx["identb"][:], in_=dram["c_identb"][:, :]))
        kb.dma("dx", q.dma_start(out=ctx["mask0"][:], in_=dram["c_mask0"][:, :]))
        kb.dma("dx", q.dma_start(out=ctx["mask128"][:], in_=dram["c_mask128"][:, :]))
    kb.sync(_loads)
    ctx["consts_tok"] = kb.tok("dx")

    def _memsets(v):
        kb.wait(v, "dve", ctx["consts_tok"])
        v.memset(ctx["ones_col"][:], 1.0)
        v.memset(ctx["ones_row"][:], 1.0)
        v.memset(ctx["ones_rowb"][:], 1.0)
        ins = v.memset(ctx["eps1"][:], LN_EPS)
        kb.inc("aux", ins)
    kb.dve(_memsets)
    ctx["ones_tok"] = kb.tok("aux")

    for seq in range(int(os.environ.get("KSEQ", "2"))):
        ctx["seq"] = seq
        xst = ctx.get("xstore_tok")

        def _xload(q, seq=seq, xst=xst):
            kb.wait(q, "sync", xst)
            for pc in range(NPC):
                kb.dma("dx", q.dma_start(out=ctx["x_sb"][:, pc * S:(pc + 1) * S],
                                         in_=dram["x_fm"][seq, pc * P:(pc + 1) * P, :]))
        kb.sync(_xload)
        ctx["x_tok"] = kb.tok("dx")
        ctx["arb0_free"] = None
        ctx["arf_free"] = None

        if "attn_a" in stages:
            try:
                stage_attn_a(ctx)
            except _SkipRest:
                ctx["x_tok"] = [ctx["x_tok"], kb.tok("pev"), kb.tok("vte"),
                                kb.tok("dw"), kb.tok("prj"), kb.tok("app")]
        if "ffn1" in stages:
            stage_ffn(ctx)
        if "attn_b" in stages:
            stage_attn_b(ctx)
        if "ffn2" in stages:
            stage_ffn(ctx)

        def _xstore(q, seq=seq):
            kb.wait(q, "sync", ctx["x_tok"])
            for pc in range(NPC):
                kb.dma("dx", q.dma_start(out=y_hbm[seq, pc * P:(pc + 1) * P, :],
                                         in_=ctx["x_sb"][:, pc * S:(pc + 1) * S]))
        kb.sync(_xstore)
        ctx["xstore_tok"] = kb.tok("dx")

    def _fin(q):
        kb.wait(q, "sync", ctx["xstore_tok"])
    kb.sync(_fin)
    kb.check_waits()
    kb.check_deadlock()
    return nc, es


def x_view(ctx, ti, tw=TT):
    return ctx["x_sb"][:].rearrange("p (c s) -> p c s", c=NPC)[:, :, ti * tw:(ti + 1) * tw]


def ar_view(t, npc, tw):
    return t[:, 0:npc * tw].rearrange("p (c s) -> p c s", c=npc)




def chain_len(ln):
    return (11, 4) if ln == "lnln" else (6, 2)


def rowslot(ctx, which, i, tw):
    t = ctx["rows"] if which == "s" else ctx["rows2"]
    return t[0:1, i * TT:i * TT + tw]


def emit_row_dve(ctx, v, ln, tw, d0, a0, p_sum, p_sq, smm_toks):
    kb = ctx["kb"]
    nD, nA = chain_len(ln)

    def S_(i):
        return rowslot(ctx, "s", i, tw)

    def T_(i):
        return rowslot(ctx, "t", i, tw)

    def dop(k, ins):
        kb.wait(v, "dve", ("rowd", d0 + k - 1))
        assert kb.cnt["rowd"] == d0 + k - 1, (kb.cnt["rowd"], d0, k)
        kb.inc("rowd", ins)

    kb.wait(v, "dve", smm_toks[0])
    dop(1, v.tensor_scalar(T_(0), p_sum[0:1, 0:tw], 1.0 / D, None, ALU.mult))
    kb.wait(v, "dve", smm_toks[1])
    dop(2, v.tensor_scalar(S_(1), p_sq[0:1, 0:tw], 1.0 / D, None, ALU.mult))
    dop(3, v.tensor_tensor(T_(1), T_(0), T_(0), ALU.mult))
    dop(4, v.tensor_tensor(S_(1), S_(1), T_(1), ALU.subtract))
    dop(5, v.tensor_scalar(S_(2), S_(1), LN_EPS, None, ALU.add))
    kb.wait(v, "dve", ("rowa", a0 + 2))
    dop(6, v.tensor_tensor(S_(0), T_(0), S_(3), ALU.mult))
    if ln == "lnln":
        dop(7, v.reciprocal(T_(2), S_(2)))
        dop(8, v.tensor_tensor(T_(3), S_(1), T_(2), ALU.mult))
        dop(9, v.tensor_scalar(T_(3), T_(3), LN_EPS, None, ALU.add))
        kb.wait(v, "dve", ("rowa", a0 + 4))
        dop(10, v.tensor_tensor(S_(1), S_(3), T_(2), ALU.mult))
        dop(11, v.tensor_tensor(S_(2), T_(0), S_(1), ALU.mult))


def emit_row_act(ctx, sc, ln, tw, d0, a0):
    kb = ctx["kb"]
    nD, nA = chain_len(ln)

    def S_(i):
        return rowslot(ctx, "s", i, tw)

    def T_(i):
        return rowslot(ctx, "t", i, tw)

    def aop(k, ins):
        assert kb.cnt["rowa"] == a0 + k - 1
        kb.inc("rowa", ins)

    kb.wait(sc, "act", ("rowd", d0 + 5))
    aop(1, sc.activation(T_(1), S_(2), AF.Ln))
    kb.wait(sc, "act", ("rowa", a0 + 1))
    aop(2, sc.activation(S_(3), T_(1), AF.Exp, scale=-0.5))
    if ln == "lnln":
        kb.wait(sc, "act", ("rowd", d0 + 9))
        aop(3, sc.activation(T_(1), T_(3), AF.Ln))
        kb.wait(sc, "act", ("rowa", a0 + 3))
        aop(4, sc.activation(T_(2), T_(1), AF.Exp, scale=-0.5))


def row_ready_val(ln, dbase, ti):
    nD, _ = chain_len(ln)
    return dbase + ti * nD + (11 if ln == "lnln" else 6)



def proj_phase(ctx, projs, ln="plain", affine_gb=None, ntile=NTT, tw=TT,
               vspec=None):
    kb = ctx["kb"]
    nc = ctx["nc"]
    arb0, arb1, arf0, arf1 = ctx["arb0"], ctx["arb1"], ctx["arf0"], ctx["arf1"]
    cst = ctx["cst"]
    nD, nA = chain_len(ln)
    napp = 3 if ln == "lnln" else 2
    x_tok = ctx["x_tok"]
    nsub = tw // P

    base = {r: kb.cnt[r] for r in ("cast", "smm", "sq", "rowd", "rowa", "bc",
                                   "app", "prj", "pev", "pev2", "vtr", "vte")}
    n_oc = sum(pr["oc"] for pr in projs)
    vproj = vspec["proj_idx"] if vspec else None
    v_oc0 = sum(pr["oc"] for pr in projs[:vproj]) if vspec else None

    eng_of = []
    for pr in projs:
        for _ in range(pr["oc"]):
            eng_of.append("pev2" if pr["bias_off"] is not None else "pev")
    per_tile = {"pev": eng_of.count("pev"), "pev2": eng_of.count("pev2")}

    def evac_tok_of(g):
        ti, oi = divmod(g, n_oc)
        e = eng_of[oi]
        ordn = ti * per_tile[e] + sum(1 for x in eng_of[:oi + 1] if x == e)
        return (e, base[e] + ordn)

    def sec_pool(g):
        for ti in range(ntile):
            kb.wait(g, "pool", x_tok, ctx.get("arb0_free"))
            if ti > 0:
                kb.wait(g, "pool", ("prj", base["prj"] + ti * n_oc),
                        ("sq", base["sq"] + ti))
            assert kb.cnt["cast"] == base["cast"] + ti
            ins = g.tensor_copy(ar_view(arb0, NPC, tw), x_view(ctx, ti, tw))
            kb.inc("cast", ins)
            if ln == "lnln":
                kb.wait(g, "pool", ("app", base["app"] + napp * ti + 2))
                go, bo = affine_gb
                for pc in range(NPC):
                    ins = g.tensor_scalar(arb0[:, pc * tw:(pc + 1) * tw],
                                          arf1[:, pc * tw:(pc + 1) * tw],
                                          cst[:, go + pc:go + pc + 1],
                                          cst[:, bo + pc:bo + pc + 1],
                                          ALU.mult, ALU.add)
                kb.inc("app", ins)

    def sec_act(sc):
        for ti in range(ntile):
            kb.wait(sc, "act", ("cast", base["cast"] + ti + 1),
                    ("smm", base["smm"] + 2 * ti + 1))
            if ti == 0:
                kb.wait(sc, "act", *ctx.get("arb1_free", []))
            assert kb.cnt["sq"] == base["sq"] + ti
            ins = sc.activation(arb1[:, 0:NPC * tw], arb0[:, 0:NPC * tw], AF.Square)
            kb.inc("sq", ins)
            emit_row_act(ctx, sc, ln, tw, base["rowd"] + ti * nD, base["rowa"] + ti * nA)
            for oi in range(n_oc):
                if eng_of[oi] == "pev2":
                    g = ti * n_oc + oi
                    kb.wait(sc, "act", ("prj", base["prj"] + g + 1))
                    pr, oc = _proj_of(projs, oi)
                    bo = pr["bias_off"]
                    rr = pr.get("rows", P)
                    if vspec and pr is projs[vproj]:
                        dst = arb1[:, oc * tw:(oc + 1) * tw]
                    else:
                        dst = pr["dst"][0:rr, oc * S + ti * tw:oc * S + (ti + 1) * tw]
                    ins = sc.activation(dst, p_c[g % 2][0:rr, :], AF.Identity,
                                        bias=cst[:, bo + oc:bo + oc + 1])
                    kb.inc("pev2", ins)

    def sec_dve(v):
        for ti in range(ntile):
            smm_t = (("smm", base["smm"] + 2 * ti + 1),
                     ("smm", base["smm"] + 2 * ti + 2))
            if ti == 0:
                kb.wait(v, "dve", *ctx.get("rows_free", []))
            emit_row_dve(ctx, v, ln, tw, base["rowd"] + ti * nD,
                         base["rowa"] + ti * nA, p_sum, p_sq, smm_t)
            kb.wait(v, "dve", ("bc", base["bc"] + ti + 1),
                    ("sq", base["sq"] + ti + 1))
            pa3 = p_a[:, None, 0:tw].broadcast_to([P, NPC, tw])
            pb3 = p_b[:, None, 0:tw].broadcast_to([P, NPC, tw])
            ins = v.tensor_tensor(ar_view(arf0, NPC, tw), x_view(ctx, ti, tw),
                                  pa3, ALU.mult)
            kb.inc("app", ins)
            dst2 = ar_view(arf1 if ln == "lnln" else arb0, NPC, tw)
            ins = v.tensor_tensor(dst2, ar_view(arf0, NPC, tw), pb3, ALU.subtract)
            kb.inc("app", ins)
            for oi in range(n_oc):
                if eng_of[oi] == "pev":
                    g = ti * n_oc + oi
                    kb.wait(v, "dve", ("prj", base["prj"] + g + 1))
                    pr, oc = _proj_of(projs, oi)
                    rr = pr.get("rows", P)
                    if vspec and pr is projs[vproj]:
                        dst = arb1[:, oc * tw:(oc + 1) * tw]
                    else:
                        dst = pr["dst"][0:rr, oc * S + ti * tw:oc * S + (ti + 1) * tw]
                    ins = v.tensor_copy(dst, p_c[g % 2][0:rr, :])
                    kb.inc("pev", ins)
            if vspec:
                vt = vspec["vtok"]
                vt5 = vt[:].rearrange("p (o k t e) -> p o k t e", o=NPC, k=NKC, t=2)
                for oc in range(NPC):
                    vg = ti * NPC + oc
                    kb.wait(v, "dve", ("vtr", base["vtr"] + vg + 1))
                    dst = vt5[:, oc, ti * nsub:(ti + 1) * nsub, :, 0:64]
                    src_ = p_t[vg % 2][:, 0:nsub * P].rearrange(
                        "p (c t e) -> p c t e", c=nsub, t=2)
                    ins = v.tensor_copy(dst, src_)
                    kb.inc("vte", ins)

    def sec_pe(t):
        for ti in range(ntile):
            kb.wait(t, "pe", ("cast", base["cast"] + ti + 1), ctx["ones_tok"])
            if ti == 0:
                kb.wait(t, "pe", *ctx.get("psum_free", []))
            else:
                kb.wait(t, "pe", ("rowd", base["rowd"] + (ti - 1) * nD + 2))
            assert kb.cnt["smm"] == base["smm"] + 2 * ti
            for pc in range(NPC):
                ins = t.matmul(p_sum[:], ctx["ones_col"][:],
                               arb0[:, pc * tw:(pc + 1) * tw],
                               start=(pc == 0), stop=(pc == NPC - 1))
            kb.inc("smm", ins)
            kb.wait(t, "pe", ("sq", base["sq"] + ti + 1))
            for pc in range(NPC):
                ins = t.matmul(p_sq[:], ctx["ones_col"][:],
                               arb1[:, pc * tw:(pc + 1) * tw],
                               start=(pc == 0), stop=(pc == NPC - 1))
            kb.inc("smm", ins)
            kb.wait(t, "pe", ("rowd", row_ready_val(ln, base["rowd"], ti)))
            if ti > 0:
                kb.wait(t, "pe", ("app", base["app"] + napp * ti))
            t.matmul(p_a[:], ctx["ones_row"][:], rowslot(ctx, "s", 3, tw),
                     start=True, stop=True)
            ins = t.matmul(p_b[:], ctx["ones_row"][:], rowslot(ctx, "s", 0, tw),
                           start=True, stop=True)
            kb.inc("bc", ins)
            kb.wait(t, "pe", ("app", base["app"] + napp * ti + napp))
            oc_i = 0
            for pr in projs:
                kb.wait(t, "pe", pr["wtok"])
                w, wk, noc = pr["w"], pr["wk"], pr["oc"]
                mw = pr.get("rows", noc * P) if noc == 1 else noc * P
                for oc in range(noc):
                    g = ti * n_oc + oc_i
                    if g >= 2:
                        kb.wait(t, "pe", evac_tok_of(g - 2))
                    cols = min(P, mw - oc * P)
                    dstp = p_c[g % 2]
                    for kc in range(wk):
                        ins = t.matmul(dstp[0:cols, :],
                                       w[:, kc * mw + oc * P:kc * mw + oc * P + cols],
                                       arb0[:, kc * tw:(kc + 1) * tw],
                                       start=(kc == 0), stop=(kc == wk - 1))
                    kb.inc("prj", ins)
                    oc_i += 1
            if vspec:
                for oc in range(NPC):
                    vg = ti * NPC + oc
                    g = ti * n_oc + v_oc0 + oc
                    kb.wait(t, "pe", evac_tok_of(g))
                    if vg >= 2:
                        kb.wait(t, "pe", ("vte", base["vte"] + vg - 1))
                    dstp = p_t[vg % 2]
                    for c in range(nsub):
                        ins = t.transpose(
                            dstp[:, c * P:(c + 1) * P],
                            arb1[:, oc * tw + c * P:oc * tw + (c + 1) * P],
                            ctx["identb"][:])
                    kb.inc("vtr", ins)

    with ExitStack() as ps:
        p_sum = ps.enter_context(nc.psum_tensor(_uq("pp_sum"), [1, tw], F32))
        p_sq = ps.enter_context(nc.psum_tensor(_uq("pp_sq"), [1, tw], F32))
        p_a = ps.enter_context(nc.psum_tensor(_uq("pp_a"), [P, tw], F32))
        p_b = ps.enter_context(nc.psum_tensor(_uq("pp_b"), [P, tw], F32))
        p_c = [ps.enter_context(nc.psum_tensor(_uq(f"pp_c{j}"), [P, tw], F32))
               for j in range(2)]
        p_t = [ps.enter_context(nc.psum_tensor(_uq(f"pp_t{j}"), [P, 512], BF16))
               for j in range(2)] if vspec else []
        kb.pool(sec_pool)
        kb.act(sec_act)
        kb.dve(sec_dve)
        kb.pe(sec_pe)

    ctx["arb0_free"] = kb.tok("prj")
    fr = [kb.tok("smm")]
    if vspec:
        fr.append(kb.tok("vtr"))
    ctx["arb1_free"] = fr
    ctx["rows_free"] = [kb.tok("bc")]
    ctx["psum_free"] = [kb.tok("rowd"), kb.tok("pev"), kb.tok("pev2"),
                        kb.tok("app"), kb.tok("vte")]
    return dict(pev=kb.tok("pev"), pev2=kb.tok("pev2"), vte=kb.tok("vte"))


def _proj_of(projs, oi):
    for pr in projs:
        if oi < pr["oc"]:
            return pr, oi
        oi -= pr["oc"]
    raise IndexError



def lat_phase(ctx, src, src_npc, projs, vspec=None, ntile=NTT, tw=TT):
    kb = ctx["kb"]
    nc = ctx["nc"]
    arb1 = ctx["arb1"]
    nsub = tw // P
    base = {r: kb.cnt[r] for r in ("prj", "pev", "pev2", "vtr", "vte")}
    n_oc = sum(pr["oc"] for pr in projs)
    vproj = vspec["proj_idx"] if vspec else None
    v_oc0 = sum(pr["oc"] for pr in projs[:vproj]) if vspec else None
    eng_of = []
    for pr in projs:
        for _ in range(pr["oc"]):
            eng_of.append("pev2" if pr["bias_off"] is not None else "pev")
    per_tile = {"pev": eng_of.count("pev"), "pev2": eng_of.count("pev2")}

    def evac_tok_of(g):
        ti, oi = divmod(g, n_oc)
        e = eng_of[oi]
        ordn = ti * per_tile[e] + sum(1 for x in eng_of[:oi + 1] if x == e)
        return (e, base[e] + ordn)

    def sec_pe(t):
        for ti in range(ntile):
            if ti == 0:
                kb.wait(t, "pe", *ctx.get("psum_free", []))
                kb.wait(t, "pe", ctx.get("lat_tok"))
                for pr in projs:
                    kb.wait(t, "pe", pr.get("wtok"))
                if vspec:
                    kb.wait(t, "pe", *ctx.get("arb1_free", []))
            oc_i = 0
            for pr in projs:
                w, wk, noc = pr["w"], pr["wk"], pr["oc"]
                for oc in range(noc):
                    g = ti * n_oc + oc_i
                    if g >= 2:
                        kb.wait(t, "pe", evac_tok_of(g - 2))
                    dstp = p_c[g % 2]
                    for kc in range(wk):
                        ins = t.matmul(dstp[:],
                                       w[:, kc * (noc * P) + oc * P:
                                         kc * (noc * P) + (oc + 1) * P],
                                       src[:, kc * S + ti * tw:kc * S + (ti + 1) * tw],
                                       start=(kc == 0), stop=(kc == wk - 1))
                    kb.inc("prj", ins)
                    oc_i += 1
            if vspec:
                for oc in range(NPC):
                    vg = ti * NPC + oc
                    g = ti * n_oc + v_oc0 + oc
                    kb.wait(t, "pe", evac_tok_of(g))
                    if vg >= 2:
                        kb.wait(t, "pe", ("vte", base["vte"] + vg - 1))
                    dstp = p_t[vg % 2]
                    for c in range(nsub):
                        ins = t.transpose(
                            dstp[:, c * P:(c + 1) * P],
                            arb1[:, oc * tw + c * P:oc * tw + (c + 1) * P],
                            ctx["identb"][:])
                    kb.inc("vtr", ins)

    def sec_dve(v):
        for ti in range(ntile):
            for oi in range(n_oc):
                if eng_of[oi] == "pev":
                    g = ti * n_oc + oi
                    kb.wait(v, "dve", ("prj", base["prj"] + g + 1))
                    pr, oc = _proj_of(projs, oi)
                    if vspec and pr is projs[vproj]:
                        dst = arb1[:, oc * tw:(oc + 1) * tw]
                    else:
                        dst = pr["dst"][:, oc * S + ti * tw:oc * S + (ti + 1) * tw]
                    ins = v.tensor_copy(dst, p_c[g % 2][:])
                    kb.inc("pev", ins)
            if vspec:
                vt = vspec["vtok"]
                vt5 = vt[:].rearrange("p (o k t e) -> p o k t e", o=NPC, k=NKC, t=2)
                for oc in range(NPC):
                    vg = ti * NPC + oc
                    kb.wait(v, "dve", ("vtr", base["vtr"] + vg + 1))
                    dst = vt5[:, oc, ti * nsub:(ti + 1) * nsub, :, 0:64]
                    src_ = p_t[vg % 2][:, 0:nsub * P].rearrange(
                        "p (c t e) -> p c t e", c=nsub, t=2)
                    ins = v.tensor_copy(dst, src_)
                    kb.inc("vte", ins)

    def sec_act(sc):
        for ti in range(ntile):
            for oi in range(n_oc):
                if eng_of[oi] == "pev2":
                    g = ti * n_oc + oi
                    kb.wait(sc, "act", ("prj", base["prj"] + g + 1))
                    pr, oc = _proj_of(projs, oi)
                    bo = pr["bias_off"]
                    if vspec and pr is projs[vproj]:
                        dst = arb1[:, oc * tw:(oc + 1) * tw]
                    else:
                        dst = pr["dst"][:, oc * S + ti * tw:oc * S + (ti + 1) * tw]
                    ins = sc.activation(dst, p_c[g % 2][:], AF.Copy,
                                        bias=ctx["cst"][:, bo + oc:bo + oc + 1])
                    kb.inc("pev2", ins)

    with ExitStack() as ps:
        p_c = [ps.enter_context(nc.psum_tensor(_uq(f"pl_c{j}"), [P, tw], F32))
               for j in range(2)]
        p_t = [ps.enter_context(nc.psum_tensor(_uq(f"pl_t{j}"), [P, 512], BF16))
               for j in range(2)] if vspec else []
        kb.pe(sec_pe)
        kb.dve(sec_dve)
        if per_tile["pev2"]:
            kb.act(sec_act)

    fr = list(ctx.get("arb1_free", []))
    if vspec:
        fr.append(kb.tok("vtr"))
        ctx["arb1_free"] = fr
    ctx["psum_free"] = [kb.tok("pev"), kb.tok("pev2"), kb.tok("vte")]
    return dict(pev=kb.tok("pev"), pev2=kb.tok("pev2"), vte=kb.tok("vte"))


def vtok_ones_init(ctx, vtok, after=None):
    kb = ctx["kb"]

    def f(v):
        kb.wait(v, "dve", after)
        vt5 = vtok[:].rearrange("p (o k t e) -> p o k t e", o=NPC, k=NKC, t=2)
        ins = v.memset(vt5[:, :, :, :, 64:65], 1.0)
        kb.inc("aux", ins)
    kb.dve(f)
    return kb.tok("aux")




def attention_phase(ctx, cfg):
    kb = ctx["kb"]
    nc = ctx["nc"]
    kind = cfg["kind"]
    scale = cfg["scale"]
    shift = kind in ("a", "mha")
    mha = kind == "mha"
    pid = ctx["phase_id"] = ctx.get("phase_id", 0) + 1
    sfx = f"_{pid}"

    es_loc = ExitStack()
    qblk = es_loc.enter_context(nc.sbuf_tensor("at_q" + sfx, [P, NPC * QT], BF16)) \
        if kind != "b" else None
    obuf = None
    apart = es_loc.enter_context(nc.sbuf_tensor("at_a" + sfx, [P, 2 * 512], BF16))
    rb_sb = es_loc.enter_context(nc.sbuf_tensor("at_rb" + sfx, [64, QT], F32))
    onrows = es_loc.enter_context(nc.sbuf_tensor("at_on" + sfx, [1, 2 * QT], F32))
    mx_tmp = es_loc.enter_context(nc.sbuf_tensor("at_mx" + sfx, [P, 1], F32))
    h2b = arb1_alias = None
    mrun = es_loc.enter_context(nc.sbuf_tensor("at_mr" + sfx, [P, NH], F32)) if shift else None
    chtmp = es_loc.enter_context(nc.sbuf_tensor("at_ct" + sfx, [1, 2], F32)) if shift else None

    arb0, arb1, arf0, arf1 = ctx["arb0"], ctx["arb1"], ctx["arf0"], ctx["arf1"]
    obuf = arb0
    h2b = arb1
    if kind == "b":
        qblk = arb1
    cst = ctx["cst"]
    co = ctx["cst_off"]
    x_tok = ctx["x_tok"]
    nD, nA = chain_len("lnln")
    nDp, nAp = chain_len("plain")

    def kmax_of(qt):
        return 2 * qt + 2

    c = dict(kb.cnt)
    ev = {}

    def bump(role, key=None, n=1):
        c[role] += n
        if key is not None:
            ev[key] = c[role]
        return c[role]

    def plan_qprod(tag, qt):
        if mha:
            bump("cast", ("castx", tag, qt))
            bump("sq", ("sqx", tag, qt))
            bump("smm", ("smx1", tag, qt))
            bump("smm", ("smx2", tag, qt))
            ev[("rowd0", tag, qt)] = c["rowd"]
            ev[("rowa0", tag, qt)] = c["rowa"]
            bump("rowd", ("rowq", tag, qt), n=nD)
            bump("rowa", ("rowaq", tag, qt), n=nA)
            bump("bc", ("bch1", tag, qt))
            bump("app", ("appa", tag, qt))
            bump("app", ("appb", tag, qt))
            bump("app", ("appc", tag, qt))
        for oc in range(NPC):
            bump("prj", ("qp", tag, qt, oc))
        for oc in range(NPC):
            bump("qe", ("qe", tag, qt, oc))
        ev[("qe_all", tag, qt)] = c["qe"]

    mx_seq = []
    if shift:
        for qt in range(NQT):
            plan_qprod("p1", qt)
            ng4 = (kmax_of(qt) + 3) // 4
            for h in range(NH):
                for g in range(ng4):
                    bump("sc", ("sc1", qt, h, g))
                    bump("mx", ("mxr", qt, h, g))
                    bump("mx", ("mxm", qt, h, g))
                    mx_seq.append((qt, h, g))
        for h in range(NH):
            bump("ch", ("ch", h))
            bump("mx", ("chr", h))
            bump("mx", ("chs", h))
            bump("chp", ("chp", h))
            bump("chv", ("chv", h))
    mx_pos = {k: i for i, k in enumerate(mx_seq)}

    gi_rev = {qt: {} for qt in range(NQT)}
    for qt in range(NQT):
        if kind == "b":
            bump("hdma", ("qrd", qt), n=16)
        plan_qprod("p2", qt)
        gi = 0
        for h in range(NH):
            for g in range(qt + 1):
                bump("sc", ("sc2", qt, h, g))
                if kind == "b":
                    bump("cl", ("cl", qt, h, g))
                bump("ex", ("ex", qt, h, g))
                ev[("gi", qt, h, g)] = gi
                gi_rev[qt][gi] = (h, g)
                gi += 1
            bump("msk", ("msk", qt, h))
            for g in range(qt + 1):
                bump("avg", ("avg", qt, h, g))
            bump("on", ("ona", qt, h))
            bump("on", ("onb", qt, h))
            bump("ob", ("ob", qt, h))
            bump("oev", ("oeva", qt, h))
            bump("oev", ("oevb", qt, h))
        ev[("oev_all", qt)] = c["oev"]
        for oc in range(NPC):
            bump("yp", ("yp", qt, oc))
        if cfg["consumer"] == "resid":
            for oc in range(NPC):
                bump("cons", ("cons", qt, oc))
        else:
            bump("bc", ("bcu", qt))
            bump("cons", ("consa", qt))
            bump("cons", ("consb", qt))
            for oc in range(NPC):
                bump("cons", ("consy", qt, oc))
            bump("cons", ("consx", qt))
            bump("cast", ("castu", qt))
            bump("sq", ("squ", qt))
            bump("smm", ("smu1", qt))
            bump("smm", ("smu2", qt))
            ev[("rowd0u", qt)] = c["rowd"]
            ev[("rowa0u", qt)] = c["rowa"]
            bump("rowd", ("rowu", qt), n=nDp)
            bump("rowa", ("rowau", qt), n=nAp)
            bump("bc", ("bcl2", qt))
            bump("app", ("apph2a", qt))
            bump("app", ("apph2b", qt))
            bump("app", ("apph2c", qt))
            bump("hdma", ("hdma", qt), n=16)

    def chk(role, key):
        if role in ("app", "cast", "sq"):
            return
        assert kb.cnt[role] == ev[key], (role, key, kb.cnt[role], ev[key])

    vt4 = cfg["vtok"][:].rearrange("p (o k c) -> p o k c", o=NPC, k=NKC)
    qrq_v = cfg["qrq"][:].rearrange("p (b g s) -> p b g s", b=2, g=4) \
        if kind == "b" else None

    def vslice(h, kc):
        b = (h % 2) * 65
        return vt4[:, h // 2, kc, b:b + 65]

    def kslice(h, kc):
        b = (h % 2) * 64
        return cfg["kT"][b:b + 64, (h // 2) * S + kc * P:(h // 2) * S + (kc + 1) * P]

    def qslice(h):
        b = (h % 2) * 64
        return qblk[b:b + 64, (h // 2) * QT:(h // 2 + 1) * QT]

    def qr_base(h):
        return (h % 2) * 64 + ((h // 2) % 2) * 32

    def qe_war(tag, qt, oc):
        if oc >= 2:
            return ("qe", ev[("qe", tag, qt, oc - 2)])
        if tag == "p1":
            if qt > 0:
                return ("qe", ev[("qe", "p1", qt - 1, oc + 6)])
            return None
        if qt == 0:
            if shift:
                return ("qe", ev[("qe_all", "p1", NQT - 1)])
            return None
        if mha:
            return None
        return ("cons", ev[("cons", qt - 1, 6 + oc)])

    def ps_war(qt, gi):
        if gi >= 2:
            h2, g2 = gi_rev[qt][gi - 2]
            return ("cl", ev[("cl", qt, h2, g2)]) if kind == "b" \
                else ("ex", ev[("ex", qt, h2, g2)])
        if qt > 0:
            if mha:
                return ("app", ev[("apph2b", qt - 1)])
            if kind == "b":
                return ("cl", ev[("cl", qt - 1, NH - 1, qt - 1)])
            return ("ex", ev[("ex", qt - 1, NH - 1, qt - 1)])
        return None

    def apart_war(qt, gi):
        if gi >= 2:
            h2, g2 = gi_rev[qt][gi - 2]
            return ("avg", ev[("avg", qt, h2, g2)])
        if qt > 0:
            return ("avg", ev[("avg", qt - 1, NH - 1, qt - 1)])
        return None

    def po_war(qt, h):
        if h > 0:
            return ("oev", ev[("oevb", qt, h - 1)])
        if qt > 0:
            return ("oev", ev[("oevb", qt - 1, NH - 1)])
        return None

    with ExitStack() as ps0:
        p_q = [ps0.enter_context(nc.psum_tensor(_uq(f"pa_q{j}" + sfx), [P, QT], F32))
               for j in range(2)]
        p_sum = ps0.enter_context(nc.psum_tensor(_uq("pa_su") + sfx, [1, QT], F32)) if mha else None
        p_sq = ps0.enter_context(nc.psum_tensor(_uq("pa_sq") + sfx, [1, QT], F32)) if mha else None

        def pe_qprod(t, tag, qt):
            if mha:
                kb.wait(t, "pe", ("cast", ev[("castx", tag, qt)]), ctx["ones_tok"])
                if (tag, qt) != ("p1" if shift else "p2", 0):
                    prev = ("p1", qt - 1) if (tag == "p1") else \
                           (("p2", qt - 1) if qt > 0 else ("p1", NQT - 1))
                    if prev[0] == "p2" and cfg["consumer"] == "mha":
                        kb.wait(t, "pe", ("rowd", ev[("rowu", prev[1])] - nDp + 2))
                    else:
                        kb.wait(t, "pe", ("rowd", ev[("rowq", prev[0], prev[1])] - nD + 2))
                for pc in range(NPC):
                    ins = t.matmul(p_sum[:], ctx["ones_col"][:],
                                   arb0[:, pc * QT:(pc + 1) * QT],
                                   start=(pc == 0), stop=(pc == NPC - 1))
                kb.inc("smm", ins); chk("smm", ("smx1", tag, qt))
                kb.wait(t, "pe", ("sq", ev[("sqx", tag, qt)]))
                for pc in range(NPC):
                    ins = t.matmul(p_sq[:], ctx["ones_col"][:],
                                   arb1[:, pc * QT:(pc + 1) * QT],
                                   start=(pc == 0), stop=(pc == NPC - 1))
                kb.inc("smm", ins); chk("smm", ("smx2", tag, qt))
                kb.wait(t, "pe", ("rowd", ev[("rowq", tag, qt)]))
                if tag == "p2" and qt > 0:
                    kb.wait(t, "pe", ("app", ev[("apph2b", qt - 1)]))
                if (tag, qt) != ("p1" if shift else "p2", 0):
                    prev = ("p1", qt - 1) if (tag == "p1") else \
                           (("p2", qt - 1) if qt > 0 else ("p1", NQT - 1))
                    kb.wait(t, "pe", ("qe", ev[("qe_all", prev[0], prev[1])]))
                t.matmul(p_q[0][:], ctx["ones_row"][:], rowslot(ctx, "s", 1, QT),
                         start=True, stop=True)
                ins = t.matmul(p_q[1][:], ctx["ones_row"][:], rowslot(ctx, "s", 2, QT),
                               start=True, stop=True)
                kb.inc("bc", ins); chk("bc", ("bch1", tag, qt))
                kb.wait(t, "pe", ("app", ev[("appc", tag, qt)]), cfg["wq_tok"])
                for oc in range(NPC):
                    kb.wait(t, "pe", qe_war(tag, qt, oc))
                    for kc in range(NPC):
                        ins = t.matmul(p_q[oc % 2][:],
                                       cfg["wq"][:, kc * D + oc * P:kc * D + (oc + 1) * P],
                                       arb0[:, kc * QT:(kc + 1) * QT],
                                       start=(kc == 0), stop=(kc == NPC - 1))
                    kb.inc("prj", ins); chk("prj", ("qp", tag, qt, oc))
            else:
                kb.wait(t, "pe", cfg["qlat_tok"], cfg["wq_tok"])
                for oc in range(NPC):
                    kb.wait(t, "pe", qe_war(tag, qt, oc))
                    for kc in range(2):
                        ins = t.matmul(p_q[oc % 2][:],
                                       cfg["wq"][:, kc * D + oc * P:kc * D + (oc + 1) * P],
                                       cfg["qlat"][:, kc * S + qt * QT:kc * S + (qt + 1) * QT],
                                       start=(kc == 0), stop=(kc == 1))
                    kb.inc("prj", ins); chk("prj", ("qp", tag, qt, oc))

        def act_qprod(sc, tag, qt):
            if mha:
                kb.wait(sc, "act", ("cast", ev[("castx", tag, qt)]),
                        ("smm", ev[("smx1", tag, qt)]))
                if tag == ("p1" if shift else "p2") and qt == 0:
                    kb.wait(sc, "act", *ctx.get("arb1_free", []))
                if tag == "p2" and qt > 0:
                    kb.wait(sc, "act", ("bc", ev[("bcl2", qt - 1)]),
                            ("hdma", ev[("hdma", qt - 1)]))
                ins = sc.activation(arb1[:, 0:NPC * QT], arb0[:, 0:NPC * QT], AF.Square)
                kb.inc("sq", ins); chk("sq", ("sqx", tag, qt))
                emit_row_act(ctx, sc, "lnln", QT,
                             ev[("rowd0", tag, qt)], ev[("rowa0", tag, qt)])
            for oc in range(NPC):
                kb.wait(sc, "act", ("prj", ev[("qp", tag, qt, oc)]))
                if kind == "b" and qt == 0 and oc == 0:
                    kb.wait(sc, "act", *ctx.get("arb1_free", []))
                if oc == 0 and tag == "p2" and qt > 0:
                    kb.wait(sc, "act", ("sc", ev[("sc2", qt - 1, NH - 1, qt - 1)]))
                if oc == 0 and tag == "p1" and qt > 0 and not cfg.get("no_scores"):
                    kb.wait(sc, "act", ("sc", ev[("sc1", qt - 1, NH - 1,
                                                  (kmax_of(qt - 1) + 3) // 4 - 1)]))
                dst = qblk[:, oc * QT:(oc + 1) * QT]
                if mha:
                    ins = sc.activation(dst, p_q[oc % 2][:], AF.Identity,
                                        bias=cst[:, co["bq"] + oc:co["bq"] + oc + 1])
                else:
                    ins = sc.activation(dst, p_q[oc % 2][:], AF.Copy)
                kb.inc("qe", ins); chk("qe", ("qe", tag, qt, oc))

        def dve_qprod(v, tag, qt):
            if mha:
                if tag == "p2" and qt > 0:
                    kb.wait(v, "dve", ("bc", ev[("bcl2", qt - 1)]))
                smm_t = (("smm", ev[("smx1", tag, qt)]), ("smm", ev[("smx2", tag, qt)]))
                emit_row_dve(ctx, v, "lnln", QT, ev[("rowd0", tag, qt)],
                             ev[("rowa0", tag, qt)], p_sum, p_sq, smm_t)
                kb.wait(v, "dve", ("bc", ev[("bch1", tag, qt)]),
                        ("sq", ev[("sqx", tag, qt)]))
                pa3 = p_q[0][:, None, 0:QT].broadcast_to([P, NPC, QT])
                pb3 = p_q[1][:, None, 0:QT].broadcast_to([P, NPC, QT])
                ins = v.tensor_tensor(ar_view(arf0, NPC, QT), x_view(ctx, qt, QT),
                                      pa3, ALU.mult)
                kb.inc("app", ins); chk("app", ("appa", tag, qt))
                ins = v.tensor_tensor(ar_view(arf1, NPC, QT), ar_view(arf0, NPC, QT),
                                      pb3, ALU.subtract)
                kb.inc("app", ins); chk("app", ("appb", tag, qt))

        def pool_qprod(g, tag, qt):
            if mha:
                kb.wait(g, "pool", x_tok, ctx.get("arb0_free"))
                first = (tag == "p1" and qt == 0) or \
                        (not shift and tag == "p2" and qt == 0)
                if not first:
                    prev = ("p1", qt - 1) if tag == "p1" else \
                           (("p2", qt - 1) if qt > 0 else ("p1", NQT - 1))
                    kb.wait(g, "pool", ("prj", ev[("qp", prev[0], prev[1], NPC - 1)]),
                            ("sq", ev[("sqx", prev[0], prev[1])]))
                    if tag == "p2" and qt > 0:
                        kb.wait(g, "pool", ("smm", ev[("smu1", qt - 1)]),
                                ("sq", ev[("squ", qt - 1)]))
                ins = g.tensor_copy(ar_view(arb0, NPC, QT), x_view(ctx, qt, QT))
                kb.inc("cast", ins); chk("cast", ("castx", tag, qt))
                kb.wait(g, "pool", ("app", ev[("appb", tag, qt)]))
                for pc in range(NPC):
                    ins = g.tensor_scalar(arb0[:, pc * QT:(pc + 1) * QT],
                                          arf1[:, pc * QT:(pc + 1) * QT],
                                          cst[:, co["ln1g"] + pc:co["ln1g"] + pc + 1],
                                          cst[:, co["ln1b"] + pc:co["ln1b"] + pc + 1],
                                          ALU.mult, ALU.add)
                kb.inc("app", ins); chk("app", ("appc", tag, qt))

        if shift:
            with ExitStack() as ps1:
                p_m = [ps1.enter_context(nc.psum_tensor(_uq(f"pa_m{j}" + sfx), [P, 1024], F32))
                       for j in range(2)]

                def pe_p1(t):
                    kb.wait(t, "pe", *ctx.get("psum_free", []))
                    kb.wait(t, "pe", cfg["kT_tok"])
                    for qt in range(NQT):
                        pe_qprod(t, "p1", qt)
                        kb.wait(t, "pe", ("qe", ev[("qe_all", "p1", qt)]))
                        if cfg.get("no_scores"):
                            continue
                        kmax = kmax_of(qt)
                        ng4 = (kmax + 3) // 4
                        for h in range(NH):
                            t.ldweights(ctx["ones_col"][:])
                            for g in range(ng4):
                                nk = min(4, kmax - 4 * g)
                                i = mx_pos[(qt, h, g)]
                                if i >= 2:
                                    pq, ph, pg = mx_seq[i - 2]
                                    kb.wait(t, "pe", ("mx", ev[("mxr", pq, ph, pg)]))
                                for j in range(nk):
                                    kc = 4 * g + j
                                    ins = t.matmul(p_m[i % 2][:, j * QT:(j + 1) * QT],
                                                   kslice(h, kc), qslice(h),
                                                   start=(j % 2 == 0), stop=True,
                                                   skip_group_check=True)
                                kb.inc("sc", ins); chk("sc", ("sc1", qt, h, g))
                kb.pe(pe_p1)

                def pool_p1(g):
                    for qt in range(NQT):
                        pool_qprod(g, "p1", qt)
                kb.pool(pool_p1)

                def act_p1(sc):
                    for qt in range(NQT):
                        act_qprod(sc, "p1", qt)
                kb.act(act_p1)

                def dve_p1(v):
                    kb.wait(v, "dve", *ctx.get("attn_free", []))
                    ins = v.memset(mrun[:], -1e30)
                    kb.inc("aux", ins)
                    mr_tok = kb.tok("aux")
                    for qt in range(NQT):
                        dve_qprod(v, "p1", qt)
                        if cfg.get("no_scores"):
                            continue
                        kmax = kmax_of(qt)
                        ng4 = (kmax + 3) // 4
                        for h in range(NH):
                            for g in range(ng4):
                                nk = min(4, kmax - 4 * g)
                                i = mx_pos[(qt, h, g)]
                                kb.wait(v, "dve", ("sc", ev[("sc1", qt, h, g)]), mr_tok)
                                ins = v.reduce_max(mx_tmp[:],
                                                   p_m[i % 2][:, 0:nk * QT], axis=AX)
                                kb.inc("mx", ins); chk("mx", ("mxr", qt, h, g))
                                kb.wait(v, "dve", ("mx", ev[("mxr", qt, h, g)]))
                                ins = v.tensor_max(mrun[:, h:h + 1], mrun[:, h:h + 1],
                                                   mx_tmp[:])
                                kb.inc("mx", ins); chk("mx", ("mxm", qt, h, g))
                kb.dve(dve_p1)

            with ExitStack() as ps2:
                p_1 = ps2.enter_context(nc.psum_tensor(_uq("pa_1") + sfx, [1, P], F32))
                p_128 = ps2.enter_context(nc.psum_tensor(_uq("pa_128") + sfx, [P, 1], F32))

                def pe_ch(t):
                    if cfg.get("no_chead"):
                        return
                    lg = (kmax_of(NQT - 1) + 3) // 4 - 1
                    for h in range(NH):
                        kb.wait(t, "pe", ("mx", ev[("mxm", NQT - 1, NH - 1, lg)]))
                        if h > 0:
                            kb.wait(t, "pe", ("mx", ev[("chr", h - 1)]))
                        ins = t.transpose(p_1[:], mrun[:, h:h + 1], ctx["identf"][:])
                        kb.inc("ch", ins); chk("ch", ("ch", h))
                        kb.wait(t, "pe", ("mx", ev[("chs", h)]))
                        if h > 0:
                            kb.wait(t, "pe", ("chv", ev[("chv", h - 1)]))
                        ins = t.matmul(p_128[:], ctx["ones_row"][:], chtmp[0:1, 1:2],
                                       start=True, stop=True)
                        kb.inc("chp", ins); chk("chp", ("chp", h))
                kb.pe(pe_ch)

                def dve_ch(v):
                    if cfg.get("no_chead"):
                        return
                    for h in range(NH):
                        kb.wait(v, "dve", ("ch", ev[("ch", h)]))
                        ins = v.reduce_max(chtmp[0:1, 0:1], p_1[:], axis=AX)
                        kb.inc("mx", ins); chk("mx", ("chr", h))
                        kb.wait(v, "dve", ("mx", ev[("chr", h)]))
                        ins = v.tensor_scalar(chtmp[0:1, 1:2], chtmp[0:1, 0:1],
                                              -scale, None, ALU.mult)
                        kb.inc("mx", ins); chk("mx", ("chs", h))
                        kb.wait(v, "dve", ("chp", ev[("chp", h)]))
                        ins = v.tensor_copy(ctx["chead"][:, h:h + 1], p_128[:])
                        kb.inc("chv", ins); chk("chv", ("chv", h))
                kb.dve(dve_ch)

        p1_only = cfg.get("p1_only")
        with ExitStack() as ps3:
            p_s = [ps3.enter_context(nc.psum_tensor(_uq(f"pa_s{j}" + sfx), [P, 512], F32))
                   for j in range(2)]
            p_o = ps3.enter_context(nc.psum_tensor(_uq("pa_o") + sfx, [65, QT], F32))
            p_bc = ps3.enter_context(nc.psum_tensor(_uq("pa_bc") + sfx, [64, QT], F32))

            def pe_p2(t):
                if p1_only:
                    return
                if not shift:
                    kb.wait(t, "pe", *ctx.get("psum_free", []))
                    kb.wait(t, "pe", cfg["kT_tok"])
                kb.wait(t, "pe", cfg["vtok_tok"], cfg["wout_tok"])
                if kind == "b":
                    kb.wait(t, "pe", cfg["qr_tok"])
                if shift:
                    kb.wait(t, "pe", ("chv", ev[("chv", NH - 1)]))
                for qt in range(NQT):
                    pe_qprod(t, "p2", qt)
                    kb.wait(t, "pe", ("qe", ev[("qe_all", "p2", qt)]))
                    if kind == "b":
                        kb.wait(t, "pe", ("hdma", ev[("qrd", qt)]))
                    ngl = qt + 1
                    for h in range(NH):
                        def emit_scores(g):
                            gi = ev[("gi", qt, h, g)]
                            kb.wait(t, "pe", ps_war(qt, gi))
                            for j in range(2):
                                kc = 2 * g + j
                                ins = t.matmul(p_s[gi % 2][:, j * QT:(j + 1) * QT],
                                               kslice(h, kc), qslice(h),
                                               start=(j == 0),
                                               stop=(kind != "b"),
                                               skip_group_check=True)
                                if kind == "b":
                                    bb = qr_base(h)
                                    ins = t.matmul(
                                        p_s[gi % 2][:, j * QT:(j + 1) * QT],
                                        cfg["kr4"][bb:bb + 32, kc * P:(kc + 1) * P],
                                        qrq_v[bb:bb + 32, qt % 2, h // 4, :],
                                        start=False, stop=True,
                                        skip_group_check=True,
                                        tile_position=(bb, 0))
                            kb.inc("sc", ins); chk("sc", ("sc2", qt, h, g))

                        def emit_av(g):
                            gi = ev[("gi", qt, h, g)]
                            kb.wait(t, "pe", ("ex", ev[("ex", qt, h, g)]))
                            if g == ngl - 1:
                                kb.wait(t, "pe", ("msk", ev[("msk", qt, h)]))
                            for j in range(2):
                                kc = 2 * g + j
                                ins = t.matmul(p_o[:], vslice(h, kc),
                                               apart[:, (gi % 2) * 512 + j * QT:
                                                     (gi % 2) * 512 + (j + 1) * QT],
                                               start=(g == 0 and j == 0),
                                               stop=(g == ngl - 1 and j == 1))
                            kb.inc("avg", ins); chk("avg", ("avg", qt, h, g))

                        kb.wait(t, "pe", po_war(qt, h))
                        for g in range(ngl):
                            emit_scores(g)
                            if g >= 1:
                                emit_av(g - 1)
                        emit_av(ngl - 1)
                        kb.wait(t, "pe", ("on", ev[("onb", qt, h)]))
                        ins = t.matmul(p_bc[:], ctx["ones_row"][0:1, 0:64],
                                       onrows[0:1, QT:2 * QT], start=True, stop=True)
                        kb.inc("ob", ins); chk("ob", ("ob", qt, h))
                    if mha:
                        kb.wait(t, "pe", ("ex", ev[("ex", qt, NH - 1, qt)]))
                        t.matmul(p_s[0][:, 0:QT], ctx["ones_row"][:],
                                 rowslot(ctx, "s", 3, QT), start=True, stop=True)
                        ins = t.matmul(p_s[0][:, QT:2 * QT], ctx["ones_row"][:],
                                       rowslot(ctx, "s", 0, QT), start=False, stop=True,
                                       skip_group_check=True)
                        kb.inc("bc", ins); chk("bc", ("bcu", qt))
                    kb.wait(t, "pe", ("oev", ev[("oev_all", qt)]))
                    for oc in range(NPC):
                        if oc >= 2:
                            key = ("cons", qt, oc - 2) if cfg["consumer"] == "resid" \
                                else ("consy", qt, oc - 2)
                            kb.wait(t, "pe", ("cons", ev[key]))
                        for kc in range(NPC):
                            ins = t.matmul(p_q[oc % 2][:],
                                           cfg["wout"][:, kc * D + oc * P:kc * D + (oc + 1) * P],
                                           obuf[:, kc * QT:(kc + 1) * QT],
                                           start=(kc == 0), stop=(kc == NPC - 1))
                        kb.inc("yp", ins); chk("yp", ("yp", qt, oc))
                    if mha:
                        kb.wait(t, "pe", ("cast", ev[("castu", qt)]))
                        for pc in range(NPC):
                            ins = t.matmul(p_sum[:], ctx["ones_col"][:],
                                           arb0[:, pc * QT:(pc + 1) * QT],
                                           start=(pc == 0), stop=(pc == NPC - 1))
                        kb.inc("smm", ins); chk("smm", ("smu1", qt))
                        kb.wait(t, "pe", ("sq", ev[("squ", qt)]))
                        for pc in range(NPC):
                            ins = t.matmul(p_sq[:], ctx["ones_col"][:],
                                           arb1[:, pc * QT:(pc + 1) * QT],
                                           start=(pc == 0), stop=(pc == NPC - 1))
                        kb.inc("smm", ins); chk("smm", ("smu2", qt))
                        kb.wait(t, "pe", ("rowd", ev[("rowu", qt)]))
                        t.matmul(p_s[1][:, 0:QT], ctx["ones_row"][:],
                                 rowslot(ctx, "s", 3, QT), start=True, stop=True)
                        ins = t.matmul(p_s[1][:, QT:2 * QT], ctx["ones_row"][:],
                                       rowslot(ctx, "s", 0, QT), start=False, stop=True,
                                       skip_group_check=True)
                        kb.inc("bc", ins); chk("bc", ("bcl2", qt))
            kb.pe(pe_p2)

            def act_p2(sc):
                if p1_only:
                    return
                if shift:
                    kb.wait(sc, "act", ("chv", ev[("chv", NH - 1)]))
                for qt in range(NQT):
                    act_qprod(sc, "p2", qt)
                    for h in range(NH):
                        for g in range(qt + 1):
                            gi = ev[("gi", qt, h, g)]
                            if kind == "b":
                                kb.wait(sc, "act", ("cl", ev[("cl", qt, h, g)]))
                                kb.wait(sc, "act", apart_war(qt, gi))
                                ins = sc.activation(
                                    apart[:, (gi % 2) * 512:(gi % 2) * 512 + 512],
                                    arf0[:, (gi % 2) * 512:(gi % 2) * 512 + 512],
                                    AF.Exp, scale=scale)
                            else:
                                kb.wait(sc, "act", ("sc", ev[("sc2", qt, h, g)]))
                                kb.wait(sc, "act", apart_war(qt, gi))
                                ins = sc.activation(
                                    apart[:, (gi % 2) * 512:(gi % 2) * 512 + 512],
                                    p_s[gi % 2][:], AF.Exp,
                                    bias=ctx["chead"][:, h:h + 1], scale=scale)
                            kb.inc("ex", ins); chk("ex", ("ex", qt, h, g))
                    if mha:
                        kb.wait(sc, "act", ("cast", ev[("castu", qt)]),
                                ("smm", ev[("smu1", qt)]),
                                ("bc", ev[("bcu", qt)]))
                        ins = sc.activation(arb1[:, 0:NPC * QT], arb0[:, 0:NPC * QT],
                                            AF.Square)
                        kb.inc("sq", ins); chk("sq", ("squ", qt))
                        emit_row_act(ctx, sc, "plain", QT,
                                     ev[("rowd0u", qt)], ev[("rowa0u", qt)])
            kb.act(act_p2)

            def dve_p2(v):
                if p1_only:
                    return
                for qt in range(NQT):
                    dve_qprod(v, "p2", qt)
                    for h in range(NH):
                        if kind == "b":
                            for g in range(qt + 1):
                                gi = ev[("gi", qt, h, g)]
                                kb.wait(v, "dve", ("sc", ev[("sc2", qt, h, g)]))
                                if gi >= 2:
                                    h2, g2 = gi_rev[qt][gi - 2]
                                    kb.wait(v, "dve", ("ex", ev[("ex", qt, h2, g2)]))
                                elif qt > 0:
                                    kb.wait(v, "dve", ("ex", ev[("ex", qt - 1, NH - 1, qt - 1)]))
                                ins = v.tensor_scalar(
                                    arf0[:, (gi % 2) * 512:(gi % 2) * 512 + 512],
                                    p_s[gi % 2][:], -80.0, 80.0, ALU.max, ALU.min)
                                kb.inc("cl", ins); chk("cl", ("cl", qt, h, g))
                        kb.wait(v, "dve", ("avg", ev[("avg", qt, h, qt)]))
                        if qt > 0 and h == 0:
                            kb.wait(v, "dve", ("yp", ev[("yp", qt - 1, NPC - 1)]))
                        ins = v.tensor_scalar(onrows[0:1, 0:QT], p_o[64:65, 0:QT],
                                              1e-30, None, ALU.add)
                        kb.inc("on", ins); chk("on", ("ona", qt, h))
                        kb.wait(v, "dve", ("on", ev[("ona", qt, h)]))
                        if h > 0:
                            kb.wait(v, "dve", ("ob", ev[("ob", qt, h - 1)]))
                        elif qt > 0:
                            kb.wait(v, "dve", ("ob", ev[("ob", qt - 1, NH - 1)]))
                        ins = v.reciprocal(onrows[0:1, QT:2 * QT], onrows[0:1, 0:QT])
                        kb.inc("on", ins); chk("on", ("onb", qt, h))
                        kb.wait(v, "dve", ("ob", ev[("ob", qt, h)]))
                        if qt == 0 and h == 0:
                            kb.wait(v, "dve", ctx.get("arb0_free"))
                        ins = v.tensor_copy(rb_sb[:], p_bc[:])
                        kb.inc("oev", ins); chk("oev", ("oeva", qt, h))
                        kb.wait(v, "dve", ("oev", ev[("oeva", qt, h)]))
                        b = (h % 2) * 64
                        ins = v.tensor_tensor(
                            obuf[b:b + 64, (h // 2) * QT:(h // 2 + 1) * QT],
                            p_o[0:64, 0:QT], rb_sb[:], ALU.mult)
                        kb.inc("oev", ins); chk("oev", ("oevb", qt, h))
                    if cfg["consumer"] == "resid":
                        xv = x_view(ctx, qt, QT)
                        for oc in range(NPC):
                            kb.wait(v, "dve", ("yp", ev[("yp", qt, oc)]))
                            ins = v.tensor_tensor(xv[:, oc, :], xv[:, oc, :],
                                                  p_q[oc % 2][:], ALU.add)
                            kb.inc("cons", ins); chk("cons", ("cons", qt, oc))
                    else:
                        kb.wait(v, "dve", ("bc", ev[("bcu", qt)]))
                        pa3 = p_s[0][:, None, 0:QT].broadcast_to([P, NPC, QT])
                        pb3 = p_s[0][:, None, QT:2 * QT].broadcast_to([P, NPC, QT])
                        ins = v.tensor_tensor(ar_view(arf0, NPC, QT),
                                              x_view(ctx, qt, QT), pa3, ALU.mult)
                        kb.inc("cons", ins); chk("cons", ("consa", qt))
                        ins = v.tensor_tensor(ar_view(arf1, NPC, QT),
                                              ar_view(arf0, NPC, QT), pb3, ALU.subtract)
                        kb.inc("cons", ins); chk("cons", ("consb", qt))
                        for oc in range(NPC):
                            kb.wait(v, "dve", ("yp", ev[("yp", qt, oc)]))
                            ins = v.scalar_tensor_tensor(
                                arf1[:, oc * QT:(oc + 1) * QT],
                                p_q[oc % 2][:], cst[:, co["bp"] + oc:co["bp"] + oc + 1],
                                arf1[:, oc * QT:(oc + 1) * QT], ALU.add, ALU.add)
                            kb.inc("cons", ins); chk("cons", ("consy", qt, oc))
                        xv = x_view(ctx, qt, QT)
                        ins = v.tensor_tensor(xv, xv, ar_view(arf1, NPC, QT), ALU.add)
                        kb.inc("cons", ins); chk("cons", ("consx", qt))
                        smm_t = (("smm", ev[("smu1", qt)]), ("smm", ev[("smu2", qt)]))
                        emit_row_dve(ctx, v, "plain", QT, ev[("rowd0u", qt)],
                                     ev[("rowa0u", qt)], p_sum, p_sq, smm_t)
                        kb.wait(v, "dve", ("bc", ev[("bcl2", qt)]))
                        pa3 = p_s[1][:, None, 0:QT].broadcast_to([P, NPC, QT])
                        pb3 = p_s[1][:, None, QT:2 * QT].broadcast_to([P, NPC, QT])
                        ins = v.tensor_tensor(ar_view(arf0, NPC, QT),
                                              ar_view(arf1, NPC, QT), pa3, ALU.mult)
                        kb.inc("app", ins); chk("app", ("apph2a", qt))
                        ins = v.tensor_tensor(ar_view(arf0, NPC, QT),
                                              ar_view(arf0, NPC, QT), pb3, ALU.subtract)
                        kb.inc("app", ins); chk("app", ("apph2b", qt))
            kb.dve(dve_p2)

            def pool_p2(g):
                if p1_only:
                    return
                for qt in range(NQT):
                    pool_qprod(g, "p2", qt)
                    for h in range(NH):
                        g_last = qt
                        gi = ev[("gi", qt, h, g_last)]
                        kb.wait(g, "pool", ("ex", ev[("ex", qt, h, g_last)]))
                        base_off = (gi % 2) * 512
                        g.tensor_tensor(apart[:, base_off:base_off + QT],
                                        apart[:, base_off:base_off + QT],
                                        ctx["mask0"][:], ALU.mult)
                        ins = g.tensor_tensor(apart[:, base_off + QT:base_off + 2 * QT],
                                              apart[:, base_off + QT:base_off + 2 * QT],
                                              ctx["mask128"][:], ALU.mult)
                        kb.inc("msk", ins); chk("msk", ("msk", qt, h))
                    if mha:
                        kb.wait(g, "pool", ("cons", ev[("consx", qt)]))
                        ins = g.tensor_copy(ar_view(arb0, NPC, QT),
                                            ar_view(arf1, NPC, QT))
                        kb.inc("cast", ins); chk("cast", ("castu", qt))
                        kb.wait(g, "pool", ("app", ev[("apph2b", qt)]))
                        if qt > 0:
                            kb.wait(g, "pool", ("hdma", ev[("hdma", qt - 1)]))
                        for pc in range(NPC):
                            ins = g.tensor_scalar(h2b[:, pc * QT:(pc + 1) * QT],
                                                  arf0[:, pc * QT:(pc + 1) * QT],
                                                  cst[:, co["ln2g"] + pc:co["ln2g"] + pc + 1],
                                                  cst[:, co["ln2b"] + pc:co["ln2b"] + pc + 1],
                                                  ALU.mult, ALU.add)
                        kb.inc("app", ins); chk("app", ("apph2c", qt))
            kb.pool(pool_p2)

            if kind == "b":
                qr4 = ctx["qr_dram"].rearrange("p (g s) -> p g s", g=4)

                def sync_qr(q):
                    for qt in range(NQT):
                        kb.wait(q, "sync", cfg["qr_tok"])
                        if qt >= 2:
                            kb.wait(q, "sync", ("sc", ev[("sc2", qt - 2, NH - 1, qt - 2)]))
                        bi = qt % 2
                        dst = qrq_v[:, bi, :, :]
                        ins = q.dma_start(out=dst,
                                          in_=qr4[:, :, qt * QT:(qt + 1) * QT])
                        kb.inc("hdma", ins, 16); chk("hdma", ("qrd", qt))
                kb.sync(sync_qr)

            if mha:
                h2d = ctx["h2_dram"].rearrange("(c p) s -> c p s", p=P)

                def sync_p2(q):
                    for qt in range(NQT):
                        kb.wait(q, "sync", ("app", ev[("apph2c", qt)]))
                        ins = q.dma_start(
                            out=h2d[:, :, qt * QT:(qt + 1) * QT].transpose([1, 0, 2]),
                            in_=ar_view(h2b, NPC, QT))
                        kb.inc("hdma", ins, 16); chk("hdma", ("hdma", qt))
                kb.sync(sync_p2)

    es_loc.close()
    if not cfg.get("p1_only"):
        for r in ("sc", "ex", "cl", "avg", "on", "ob", "oev", "yp", "cons", "msk",
                  "qe", "prj", "bc", "app", "cast", "sq", "smm", "rowd", "rowa",
                  "mx", "ch", "chp", "chv", "hdma"):
            assert kb.cnt[r] == c[r], (r, kb.cnt[r], c[r])
    ctx["psum_free"] = [kb.tok("cons"), kb.tok("oev"), kb.tok("qe"),
                        kb.tok("ex"), kb.tok("app"), kb.tok("rowd"),
                        kb.tok("cl"), kb.tok("mx")]
    ctx["attn_free"] = [kb.tok("ex"), kb.tok("cl"), kb.tok("avg")]
    if mha:
        ctx["arb0_free"] = kb.tok("smm")
        ctx["arb1_free"] = [kb.tok("smm")]
        ctx["rows_free"] = [kb.tok("bc")]
    return dict(done=kb.tok("cons"), h2_tok=kb.tok("hdma") if mha else None)



class _SkipRest(Exception):
    pass


def stage_attn_a(ctx):
    kb = ctx["kb"]
    nc = ctx["nc"]
    seq = ctx["seq"]
    sfx = f"_aa{seq}"
    stage_barrier = ctx["x_tok"]

    with ExitStack() as ses:
        cq = ses.enter_context(nc.sbuf_tensor("cq" + sfx, [P, 2 * S], BF16))
        kT = ses.enter_context(nc.sbuf_tensor("kT" + sfx, [P, NPC * S], BF16))
        vtok = ses.enter_context(nc.sbuf_tensor("vt" + sfx, [P, NH * NKC * 65], BF16))

        with ExitStack() as s2:
            ckv = s2.enter_context(nc.sbuf_tensor("ckv" + sfx, [P, 2 * S], BF16))
            with ExitStack() as sw:
                w_dq = sw.enter_context(nc.sbuf_tensor("wdq" + sfx, [P, NPC * DC], BF16))
                w_dkv = sw.enter_context(nc.sbuf_tensor("wdkv" + sfx, [P, NPC * DC], BF16))

                def loads(g):
                    load_weight(kb, g, w_dq, ctx["dram"]["aa_dq"], free_tok=stage_barrier)
                    load_weight(kb, g, w_dkv, ctx["dram"]["aa_dkv"])
                kb.pool(loads)
                wtok = kb.tok("dw")
                proj_phase(ctx, [
                    dict(w=w_dq, wk=NPC, oc=2, dst=cq, bias_off=None, wtok=wtok),
                    dict(w=w_dkv, wk=NPC, oc=2, dst=ckv, bias_off=None, wtok=wtok),
                ], ln="plain")
            lat_barrier = kb.tok("prj")

            with ExitStack() as sw:
                w_uk = sw.enter_context(nc.sbuf_tensor("wuk" + sfx, [P, 2 * D], BF16))
                w_uv = sw.enter_context(nc.sbuf_tensor("wuv" + sfx, [P, 2 * D], BF16))

                def loads2(g):
                    load_weight(kb, g, w_uk, ctx["dram"]["aa_uk"], free_tok=lat_barrier)
                    load_weight(kb, g, w_uv, ctx["dram"]["aa_uv"])
                if KSUB != "proj":
                    kb.pool(loads2)
                wtok2 = kb.tok("dw")
                ctx["lat_tok"] = kb.tok("pev")
                if KSUB == "latk":
                    r = lat_phase(ctx, ckv, 2, [
                        dict(w=w_uk, wk=2, oc=NPC, dst=kT, bias_off=None, wtok=wtok2),
                    ])
                    kT_tok = r["pev"]
                    vt_tok = None
                elif KSUB != "proj":
                    r = lat_phase(ctx, ckv, 2, [
                        dict(w=w_uk, wk=2, oc=NPC, dst=kT, bias_off=None, wtok=wtok2),
                        dict(w=w_uv, wk=2, oc=NPC, dst=None, bias_off=None, wtok=wtok2),
                    ], vspec=dict(proj_idx=1, vtok=vtok))
                    ones_tok = vtok_ones_init(ctx, vtok, after=None)
                    kT_tok = r["pev"]
                    vt_tok = [r["vte"], ones_tok]
        lat_barrier2 = kb.tok("prj")

        if KSUB in ("all", "attn", "p1", "p1s", "p1q"):
            with ExitStack() as sw:
                w_uq = sw.enter_context(nc.sbuf_tensor("wuq" + sfx, [P, 2 * D], BF16))
                w_o = sw.enter_context(nc.sbuf_tensor("wo" + sfx, [P, NPC * D], BF16))

                def loads3(g):
                    load_weight(kb, g, w_uq, ctx["dram"]["aa_uq"], free_tok=lat_barrier2)
                    load_weight(kb, g, w_o, ctx["dram"]["aa_o"])
                kb.pool(loads3)
                wtok3 = kb.tok("dw")

                attention_phase(ctx, dict(
                    kind="a", scale=1.0 / 8.0, consumer="resid",
                    p1_only=(KSUB in ("p1", "p1s", "p1q")),
                    no_chead=(KSUB in ("p1s", "p1q")),
                    no_scores=(KSUB == "p1q"),
                    kT=kT, kT_tok=kT_tok, vtok=vtok, vtok_tok=vt_tok,
                    qlat=cq, qlat_tok=kb.tok("pev"),
                    wq=w_uq, wq_tok=wtok3, wout=w_o, wout_tok=wtok3,
                ))
    if KSUB in ("p1", "p1s", "p1q"):
        ctx["x_tok"] = [ctx["x_tok"], kb.tok("chv"), kb.tok("qe"), kb.tok("dw")]
    elif KSUB in ("all", "attn"):
        ctx["x_tok"] = kb.tok("cons")
    else:
        ctx["x_tok"] = [ctx["x_tok"], kb.tok("pev"), kb.tok("pev2"),
                        kb.tok("vte"), kb.tok("dw")]


def stage_ffn(ctx):
    kb = ctx["kb"]
    nc = ctx["nc"]
    co = ctx["cst_off"]
    stage_barrier = ctx["x_tok"]

    with ExitStack() as ses:
        kT = ses.enter_context(nc.sbuf_tensor(_uq("fkT"), [P, NPC * S], BF16))
        vtok = ses.enter_context(nc.sbuf_tensor(_uq("fvt"), [P, NH * NKC * 65], BF16))

        with ExitStack() as sw:
            w_k = sw.enter_context(nc.sbuf_tensor(_uq("fwk"), [P, NPC * D], BF16))
            w_v = sw.enter_context(nc.sbuf_tensor(_uq("fwv"), [P, NPC * D], BF16))

            def loads(g):
                load_weight(kb, g, w_k, ctx["dram"]["f_wk"], free_tok=stage_barrier)
                load_weight(kb, g, w_v, ctx["dram"]["f_wv"])
            kb.pool(loads)
            wtok = kb.tok("dw")
            proj_phase(ctx, [
                dict(w=w_k, wk=NPC, oc=NPC, dst=kT, bias_off=co["bk"], wtok=wtok),
                dict(w=w_v, wk=NPC, oc=NPC, dst=None, bias_off=co["bv"], wtok=wtok),
            ], ln="lnln", affine_gb=(co["ln1g"], co["ln1b"]),
                vspec=dict(proj_idx=1, vtok=vtok))
            ones_tok = vtok_ones_init(ctx, vtok, after=stage_barrier)
        kT_tok = [kb.tok("pev2")]
        vt_tok = [kb.tok("vte"), ones_tok]
        proj_barrier = kb.tok("prj")
        if KSUB == "fproj":
            ctx["x_tok"] = [ctx["x_tok"], kb.tok("pev2"), kb.tok("vte"),
                            kb.tok("dw"), kb.tok("prj"), kb.tok("app"), ones_tok]
            return

        with ExitStack() as sw:
            w_q = sw.enter_context(nc.sbuf_tensor(_uq("fwq"), [P, NPC * D], BF16))
            w_p = sw.enter_context(nc.sbuf_tensor(_uq("fwp"), [P, NPC * D], BF16))

            def loads2(g):
                load_weight(kb, g, w_q, ctx["dram"]["f_wq"], free_tok=proj_barrier)
                load_weight(kb, g, w_p, ctx["dram"]["f_wp"])
            kb.pool(loads2)
            wtok2 = kb.tok("dw")

            attention_phase(ctx, dict(
                kind="mha", scale=1.0 / 8.0, consumer="mha",
                kT=kT, kT_tok=kT_tok, vtok=vtok, vtok_tok=vt_tok,
                wq=w_q, wq_tok=wtok2, wout=w_p, wout_tok=wtok2,
                p1_only=(KSUB in ("fp1", "fp1s", "fq")),
                no_chead=(KSUB in ("fp1s", "fq")),
                no_scores=(KSUB == "fq"),
            ))
        if KSUB in ("fp1", "fp1s", "fq"):
            ctx["x_tok"] = [ctx["x_tok"], kb.tok("chv"), kb.tok("qe"), kb.tok("dw"),
                            kb.tok("app")]
            return
        ctx["x_tok"] = kb.tok("cons")
        h2_tok = kb.tok("hdma")

    if KSUB == "mha":
        return
    with ExitStack() as ses:
        w8b = ses.enter_context(nc.sbuf_tensor(_uq("w8b"), [NEXP, S], BF16))
        routing_phase(ctx, w8b, h2_tok, ctx["x_tok"])
        if KSUB == "rout":
            ctx["x_tok"] = [ctx["x_tok"], kb.tok("rp"), kb.tok("dw")]
            return
        ctx["moe_h2_tok"] = h2_tok
        moe_phase(ctx, w8b)
    ctx["x_tok"] = kb.tok("yev")


def routing_phase(ctx, w8b, h2_tok, pe_barrier):
    kb = ctx["kb"]
    nc = ctx["nc"]
    co = ctx["cst_off"]
    cst = ctx["cst"]
    NTC = S // P
    NRP = 17

    base = {r: kb.cnt[r] for r in ("gt", "rp", "rp2", "hdma", "dw")}

    with ExitStack() as ps:
        h2c = ps.enter_context(nc.sbuf_tensor(_uq("h2c"), [P, NPC * P * 2], BF16))
        gw = ps.enter_context(nc.sbuf_tensor(_uq("gw"), [P, NPC * NEXP], BF16))
        lg = ps.enter_context(nc.sbuf_tensor(_uq("lg"), [P, NEXP], F32))
        m8 = ps.enter_context(nc.sbuf_tensor(_uq("m8"), [P, NEXP], F32))
        i1 = ps.enter_context(nc.sbuf_tensor(_uq("i1"), [P, NEXP], F32))
        i2 = ps.enter_context(nc.sbuf_tensor(_uq("i2"), [P, NEXP], F32))
        w_tm = ps.enter_context(nc.sbuf_tensor(_uq("wtm"), [P, NEXP], F32))
        tcl = ps.enter_context(nc.sbuf_tensor(_uq("tcl"), [P, 8], F32))
        p_l = ps.enter_context(nc.psum_tensor(_uq("p_l"), [P, NEXP], F32))
        p_w = ps.enter_context(nc.psum_tensor(_uq("p_w"), [NEXP, P], F32))

        def loads(g):
            load_weight(kb, g, gw, ctx["dram"]["f_gw"], free_tok=pe_barrier)
        kb.pool(loads)
        gw_tok = kb.tok("dw")

        h2d = ctx["h2_dram"].rearrange("(c p) s -> c p s", p=P)

        def sec_sync(q):
            for tc in range(NTC):
                kb.wait(q, "sync", h2_tok)
                bi = tc % 2
                if tc >= 2:
                    kb.wait(q, "sync", ("gt", base["gt"] + 2 * (tc - 2) + 1))
                ins = q.dma_start(out=h2c[:, bi * NPC * P:(bi + 1) * NPC * P].rearrange(
                    "p (c e) -> p c e", c=NPC),
                    in_=h2d[:, :, tc * P:(tc + 1) * P].transpose([1, 0, 2]))
                kb.inc("hdma", ins, 16)
        kb.sync(sec_sync)

        def sec_pe(t):
            kb.wait(t, "pe", *ctx.get("psum_free", []))
            kb.wait(t, "pe", pe_barrier, gw_tok, ctx["ones_tok"])
            for tc in range(NTC):
                bi = tc % 2
                kb.wait(t, "pe", ("hdma", base["hdma"] + 16 * (tc + 1)))
                if tc >= 1:
                    kb.wait(t, "pe", ("rp", base["rp"] + (tc - 1) * NRP + 1))
                for kc in range(NPC):
                    ins = t.matmul(p_l[:],
                                   h2c[:, bi * NPC * P + kc * P:bi * NPC * P + (kc + 1) * P],
                                   gw[:, kc * NEXP:(kc + 1) * NEXP],
                                   start=(kc == 0), stop=(kc == NPC - 1))
                kb.inc("gt", ins)
                kb.wait(t, "pe", ("rp", base["rp"] + tc * NRP + 16))
                if tc >= 1:
                    kb.wait(t, "pe", ("rp", base["rp"] + (tc - 1) * NRP + 17))
                ins = t.transpose(p_w[:], w_tm[:], ctx["identf"][:])
                kb.inc("gt", ins)
        kb.pe(sec_pe)

        def sec_act(sc):
            for tc in range(NTC):
                kb.wait(sc, "act", ("rp", base["rp"] + tc * NRP + 3))
                ins = sc.activation(tcl[:, 1:2], tcl[:, 0:1], AF.Exp)
                kb.inc("rp2", ins)
        kb.act(sec_act)

        def sec_dve(v):
            for tc in range(NTC):
                d0 = base["rp"] + tc * NRP

                def dop(k, ins):
                    assert kb.cnt["rp"] == d0 + k, (kb.cnt["rp"], d0, k)
                kb.wait(v, "dve", ("gt", base["gt"] + 2 * tc + 1), ctx["consts_tok"])

                def w_(k):
                    kb.wait(v, "dve", ("rp", d0 + k))
                kb.inc("rp", v.tensor_tensor(lg[:], p_l[:],
                                             cst[:, co["gateb"]:co["gateb"] + 8], ALU.add))
                w_(1)
                kb.inc("rp", v.max(out=m8[:], in_=lg[:]))
                w_(2)
                kb.inc("rp", v.tensor_tensor(tcl[:, 0:1], m8[:, 1:2], m8[:, 0:1],
                                             ALU.subtract))
                kb.wait(v, "dve", ("rp2", base["rp2"] + tc + 1))
                kb.inc("rp", v.tensor_scalar(tcl[:, 2:3], tcl[:, 1:2], 1.0, None, ALU.add))
                w_(4)
                kb.inc("rp", v.reciprocal(tcl[:, 3:4], tcl[:, 2:3]))
                w_(5)
                kb.inc("rp", v.tensor_tensor(tcl[:, 4:5], tcl[:, 1:2], tcl[:, 3:4],
                                             ALU.mult))
                w_(6)
                kb.inc("rp", v.tensor_scalar(i1[:], lg[:], m8[:, 0:1], None, ALU.is_equal))
                w_(7)
                kb.inc("rp", v.reduce_sum(tcl[:, 5:6], i1[:], axis=AX))
                w_(8)
                kb.inc("rp", v.reciprocal(tcl[:, 6:7], tcl[:, 5:6]))
                w_(9)
                kb.inc("rp", v.tensor_tensor(tcl[:, 7:8], tcl[:, 3:4], tcl[:, 6:7],
                                             ALU.mult))
                w_(10)
                kb.inc("rp", v.tensor_scalar(i2[:], lg[:], m8[:, 1:2], None, ALU.is_equal))
                w_(11)
                kb.inc("rp", v.reduce_sum(tcl[:, 5:6], i2[:], axis=AX))
                w_(12)
                kb.inc("rp", v.reciprocal(tcl[:, 6:7], tcl[:, 5:6]))
                w_(13)
                kb.inc("rp", v.tensor_tensor(tcl[:, 6:7], tcl[:, 4:5], tcl[:, 6:7],
                                             ALU.mult))
                w_(14)
                kb.inc("rp", v.tensor_scalar(w_tm[:], i1[:], tcl[:, 7:8], None, ALU.mult))
                w_(15)
                kb.inc("rp", v.scalar_tensor_tensor(w_tm[:], i2[:], tcl[:, 6:7], w_tm[:],
                                                    ALU.mult, ALU.add))
                kb.wait(v, "dve", ("gt", base["gt"] + 2 * tc + 2))
                kb.inc("rp", v.tensor_copy(w8b[:, tc * P:(tc + 1) * P], p_w[:]))
                assert kb.cnt["rp"] == d0 + NRP
        kb.dve(sec_dve)

        if ctx.get("rt_dbg") is not None:
            def dbg_sync(q):
                kb.wait(q, "sync", kb.tok("rp"), kb.tok("aux"))
                kb.dma("dx", q.dma_start(out=ctx["rt_dbg"][:, 0:8], in_=lg[:]))
                kb.dma("dx", q.dma_start(out=ctx["rt_dbg"][:, 8:16], in_=m8[:]))
                kb.dma("dx", q.dma_start(out=ctx["rt_dbg"][:, 16:24], in_=w_tm[:]))
                kb.dma("dx", q.dma_start(out=ctx["rt_dbg"][:, 24:32], in_=i1[:]))
            kb.sync(dbg_sync)
            wdbg = ps.enter_context(nc.sbuf_tensor(_uq("wdbg"), [NEXP, P], F32))

            def dbg_dve2(v):
                kb.wait(v, "dve", kb.tok("rp"))
                ins = v.tensor_copy(wdbg[:], w8b[:, 0:P])
                kb.inc("aux", ins)
            kb.dve(dbg_dve2)

            def dbg_sync2(q):
                kb.wait(q, "sync", kb.tok("aux"))
                kb.dma("dx", q.dma_start(out=ctx["rt_dbg2"][:, :], in_=wdbg[:]))
            kb.sync(dbg_sync2)

    ctx["psum_free"] = [kb.tok("rp")]
    return kb.tok("rp")


def moe_phase(ctx, w8b):
    kb = ctx["kb"]
    nc = ctx["nc"]
    arb0 = ctx["arb0"]
    MT = 512
    NMT = S // MT
    HHALF = DMID // 2
    lim = os.environ.get("KMOE")
    NE_R, NHF_R, NMT_R = (NEXP, 2, NMT) if not lim else tuple(
        int(x) for x in lim.split(","))

    base = {r: kb.cnt[r] for r in ("ymm", "yev", "mid", "pev", "pev2", "bc",
                                   "hdma", "dw")}
    NG_MID, NG_Y = 16, 8

    def m_of(e, hf, tl):
        return (e * 2 + hf) * NMT + tl

    def ymm_mid(e, hf, tl, mpc):
        return base["ymm"] + m_of(e, hf, tl) * (NG_MID + NG_Y) + mpc + 1

    def ymm_y(e, hf, tl, j):
        return base["ymm"] + m_of(e, hf, tl) * (NG_MID + NG_Y) + NG_MID + j + 1

    def pev2_of(e, hf, tl, mpc):
        return base["pev2"] + m_of(e, hf, tl) * NG_MID + mpc + 1

    def mid_of(e, hf, tl, mpc):
        return base["mid"] + m_of(e, hf, tl) * NG_MID + mpc + 1

    def yev_of(e, hf, tl, j):
        return base["yev"] + m_of(e, hf, tl) * NG_Y + j + 1

    def hdma_of(e, hf, tl):
        return base["hdma"] + (m_of(e, hf, tl) + 1) * 16

    def wbc_mm_of(e, tl):
        return base["bc"] + e * NMT + tl + 1

    def wbc_ev_of(e, tl):
        return base["pev"] + e * NMT + tl + 1

    with ExitStack() as ps:
        selc = ps.enter_context(nc.sbuf_tensor(_uq("selc"), [NEXP, NEXP * P], BF16))
        b2sb = ps.enter_context(nc.sbuf_tensor(_uq("b2sb"), [NEXP, D], BF16))
        b1c = ps.enter_context(nc.sbuf_tensor(_uq("b1c"), [P, NEXP * 32], F32))
        w1h = ps.enter_context(nc.sbuf_tensor(_uq("w1h"), [P, NPC * HHALF], BF16))
        w2h = ps.enter_context(nc.sbuf_tensor(_uq("w2h"), [P, (HHALF // P) * D], BF16))
        mid = ps.enter_context(nc.sbuf_tensor(_uq("mid"), [P, (HHALF // P) * MT], BF16))
        h2t = ps.enter_context(nc.sbuf_tensor(_uq("h2t"), [P, 2 * NPC * MT], BF16))
        wbcA = ctx["arf0"]
        p_mid = [ps.enter_context(nc.psum_tensor(_uq(f"pm{j}"), [P, MT], F32))
                 for j in range(2)]
        p_y = [ps.enter_context(nc.psum_tensor(_uq(f"py{j}"), [P, MT], F32))
               for j in range(2)]
        p_wb = ps.enter_context(nc.psum_tensor(_uq("pwb"), [P, MT], F32))

        rt_done = list(ctx.get("psum_free", []))

        def cloads(q):
            kb.wait(q, "sync", *rt_done)
            for e in range(NEXP):
                kb.dma("dx", q.dma_start(out=b1c[:, e * 32:(e + 1) * 32],
                                         in_=ctx["dram"]["c_b1"][e]))
        kb.sync(cloads)
        c_tok = kb.tok("dx")

        def b2load(g):
            kb.wait(g, "pool", *rt_done)
            kb.dma("dw", g.dma_start(out=selc[:], in_=ctx["dram"]["c_sel"][:, :]))
            load_weight(kb, g, b2sb, ctx["dram"]["f_b2"])
        kb.pool(b2load)
        b2_tok = kb.tok("dw")

        h2d = ctx["h2_dram"].rearrange("(c p) s -> c p s", p=P)

        w1_toks = {}
        w2_toks = {}

        def sec_pool(g):
            for e in range(NE_R):
                for hf in range(NHF_R):
                    if (e, hf) == (0, 0):
                        kb.wait(g, "pool", *rt_done)
                    if (e, hf) != (0, 0):
                        pe_, pf_ = (e, hf - 1) if hf else (e - 1, 1)
                        kb.wait(g, "pool", ("ymm", ymm_mid(pe_, pf_, NMT - 1, NG_MID - 1)))
                    for b in range(NPC):
                        ins = g.dma_start(
                            out=w1h[:, b * HHALF:(b + 1) * HHALF],
                            in_=ctx["dram"]["f_w1"][e, b * P:(b + 1) * P,
                                                    hf * HHALF:(hf + 1) * HHALF])
                        kb.dma("dw", ins)
                    w1_toks[(e, hf)] = kb.tok("dw")
                    if (e, hf) != (0, 0):
                        pe_, pf_ = (e, hf - 1) if hf else (e - 1, 1)
                        kb.wait(g, "pool", ("ymm", ymm_y(pe_, pf_, NMT - 1, NG_Y - 1)))
                    for b in range(HHALF // P):
                        ins = g.dma_start(
                            out=w2h[:, b * D:(b + 1) * D],
                            in_=ctx["dram"]["f_w2"][e, hf * HHALF + b * P:
                                                    hf * HHALF + (b + 1) * P, :])
                        kb.dma("dw", ins)
                    w2_toks[(e, hf)] = kb.tok("dw")
                    for tl in range(NMT_R):
                        for mpc in range(NG_MID):
                            kb.wait(g, "pool", ("pev2", pev2_of(e, hf, tl, mpc)),
                                    ("pev", wbc_ev_of(e, tl)))
                            if m_of(e, hf, tl) * NG_MID + mpc >= 2 * NG_MID:
                                pm = m_of(e, hf, tl) - 1
                                if mpc == 0:
                                    pe2, r = divmod(pm, 2 * NMT)
                                    hf2, tl2 = divmod(r, NMT)
                                    kb.wait(g, "pool", ("ymm", ymm_y(pe2, hf2, tl2,
                                                                    NG_Y - 1)))
                            ins = g.tensor_tensor(
                                mid[:, mpc * MT:(mpc + 1) * MT],
                                arb0[:, (mpc % 4) * MT:(mpc % 4 + 1) * MT],
                                wbcA[:, tl * MT:(tl + 1) * MT], ALU.mult)
                            kb.inc("mid", ins)
        kb.pool(sec_pool)

        def sec_sync(q):
            for e in range(NE_R):
                for hf in range(NHF_R):
                    for tl in range(NMT_R):
                        m = m_of(e, hf, tl)
                        bi = m % 2
                        kb.wait(q, "sync", ctx.get("moe_h2_tok"))
                        if m < 2:
                            kb.wait(q, "sync", *rt_done)

                        if m >= 2:
                            pe2, r = divmod(m - 2, 2 * NMT)
                            hf2, tl2 = divmod(r, NMT)
                            kb.wait(q, "sync", ("ymm", ymm_mid(pe2, hf2, tl2,
                                                              NG_MID - 1)))
                        ins = q.dma_start(
                            out=h2t[:, bi * NPC * MT:(bi + 1) * NPC * MT].rearrange(
                                "p (c e) -> p c e", c=NPC),
                            in_=h2d[:, :, tl * MT:(tl + 1) * MT].transpose([1, 0, 2]))
                        kb.inc("hdma", ins, 16)
        kb.sync(sec_sync)

        def sec_pe(t):
            kb.wait(t, "pe", *ctx.get("psum_free", []))
            kb.wait(t, "pe", c_tok, b2_tok)
            for e in range(NE_R):
                for hf in range(NHF_R):
                    for tl in range(NMT_R):
                        m = m_of(e, hf, tl)
                        bi = m % 2
                        if hf == 0:
                            kb.wait(t, "pe", ("mid", mid_of(e - 1, 1, tl, NG_MID - 1))
                                    if e > 0 else None)
                            if (e, tl) != (0, 0):
                                pe_, pt_ = (e, tl - 1) if tl else (e - 1, NMT - 1)
                                kb.wait(t, "pe", ("pev", wbc_ev_of(pe_, pt_)))
                            ins = t.matmul(p_wb[:], selc[:, e * P:(e + 1) * P],
                                           w8b[:, tl * MT:(tl + 1) * MT],
                                           start=True, stop=True)
                            kb.inc("bc", ins)
                        kb.wait(t, "pe", w1_toks[(e, hf)],
                                ("hdma", hdma_of(e, hf, tl)))
                        for mpc in range(NG_MID):
                            if mpc >= 2:
                                kb.wait(t, "pe", ("pev2", pev2_of(e, hf, tl, mpc - 2)))
                            elif m >= 1:
                                pm = m - 1
                                pe2, r = divmod(pm, 2 * NMT)
                                hf2, tl2 = divmod(r, NMT)
                                kb.wait(t, "pe", ("pev2", pev2_of(pe2, hf2, tl2,
                                                                 NG_MID - 2 + mpc)))
                            for kc in range(NPC):
                                ins = t.matmul(
                                    p_mid[mpc % 2][:],
                                    w1h[:, kc * HHALF + mpc * P:kc * HHALF + (mpc + 1) * P],
                                    h2t[:, bi * NPC * MT + kc * MT:
                                        bi * NPC * MT + (kc + 1) * MT],
                                    start=(kc == 0), stop=(kc == NPC - 1))
                            kb.inc("ymm", ins)
                        kb.wait(t, "pe", w2_toks[(e, hf)],
                                ("mid", mid_of(e, hf, tl, NG_MID - 1)))
                        for j in range(NG_Y):
                            if j >= 2:
                                kb.wait(t, "pe", ("yev", yev_of(e, hf, tl, j - 2)))
                            elif m >= 1:
                                pm = m - 1
                                pe2, r = divmod(pm, 2 * NMT)
                                hf2, tl2 = divmod(r, NMT)
                                kb.wait(t, "pe", ("yev", yev_of(pe2, hf2, tl2,
                                                               NG_Y - 2 + j)))
                            has_b2 = (e == 0 and hf == 0)
                            for kc in range(NG_MID):
                                ins = t.matmul(
                                    p_y[j % 2][:],
                                    w2h[:, kc * D + j * P:kc * D + (j + 1) * P],
                                    mid[:, kc * MT:(kc + 1) * MT],
                                    start=(kc == 0),
                                    stop=(kc == NG_MID - 1 and not has_b2),
                                    skip_group_check=True)
                            if has_b2:
                                ins = t.matmul(p_y[j % 2][:],
                                               b2sb[0:NEXP, j * P:(j + 1) * P],
                                               w8b[:, tl * MT:(tl + 1) * MT],
                                               start=False, stop=True,
                                               skip_group_check=True)
                            kb.inc("ymm", ins)
        kb.pe(sec_pe)

        def sec_act(sc):
            for e in range(NE_R):
                for hf in range(NHF_R):
                    for tl in range(NMT_R):
                        m = m_of(e, hf, tl)
                        for mpc in range(NG_MID):
                            kb.wait(sc, "act", ("ymm", ymm_mid(e, hf, tl, mpc)),
                                    c_tok)
                            gidx = m * NG_MID + mpc
                            if gidx == 0:
                                kb.wait(sc, "act", ctx.get("arb0_free"))
                            if gidx >= 4:
                                pm, pmpc = divmod(gidx - 4, NG_MID)
                                pe2, r = divmod(pm, 2 * NMT)
                                hf2, tl2 = divmod(r, NMT)
                                kb.wait(sc, "act", ("mid", mid_of(pe2, hf2, tl2, pmpc)))
                            bcol = e * 32 + hf * 16 + mpc
                            ins = sc.activation(arb0[:, (mpc % 4) * MT:(mpc % 4 + 1) * MT],
                                                p_mid[mpc % 2][:], AF.Gelu,
                                                bias=b1c[:, bcol:bcol + 1])
                            kb.inc("pev2", ins)
        kb.act(sec_act)

        def sec_dve(v):
            for e in range(NE_R):
                for hf in range(NHF_R):
                    for tl in range(NMT_R):
                        if hf == 0:
                            kb.wait(v, "dve", ("bc", wbc_mm_of(e, tl)))
                            ins = v.tensor_copy(wbcA[:, tl * MT:(tl + 1) * MT], p_wb[:])
                            kb.inc("pev", ins)
                        for j in range(NG_Y):
                            kb.wait(v, "dve", ("ymm", ymm_y(e, hf, tl, j)))
                            xs = ctx["x_sb"][:, j * S + tl * MT:j * S + (tl + 1) * MT]
                            ins = v.tensor_tensor(xs, xs, p_y[j % 2][:], ALU.add)
                            kb.inc("yev", ins)
        kb.dve(sec_dve)

        if ctx.get("moe_dbg") is not None:
            def mdump(g):
                kb.wait(g, "pool", kb.tok("mid"), kb.tok("yev"))
                kb.dma("dw", g.dma_start(out=ctx["moe_dbg"][0:P, 0:8192],
                                         in_=mid[:]))
                kb.dma("dw", g.dma_start(out=ctx["moe_dbg"][0:P, 8192:8192 + 2048],
                                         in_=wbcA[:, 0:2048]))
                kb.dma("dw", g.dma_start(out=ctx["moe_dbg"][0:P, 10240:10240 + 4096],
                                         in_=h2t[:, 0:NPC * MT]))
                kb.dma("dw", g.dma_start(out=ctx["moe_dbg"][0:NEXP, 14336:14336 + 1024],
                                         in_=selc[:]))
                kb.dma("dw", g.dma_start(out=ctx["moe_dbg"][0:NEXP, 15360:15360 + 1024],
                                         in_=w8b[:, 0:1024]))
            kb.pool(mdump)

    ctx["psum_free"] = [kb.tok("yev"), kb.tok("pev"), kb.tok("pev2"),
                        kb.tok("mid")]
    ctx["arb0_free"] = kb.tok("mid")


def rope_phase(ctx, cq, kx12, kx21, qr_sb, kr4, w12, w21):
    kb = ctx["kb"]
    nc = ctx["nc"]
    arf1 = ctx["arf1"]

    def qr_base(h):
        return (h % 2) * 64 + ((h // 2) % 2) * 32

    base = {r: kb.cnt[r] for r in ("gt", "rp", "dx")}
    NT2 = S // TT

    with ExitStack() as ps:
        cc = ps.enter_context(nc.sbuf_tensor(_uq("cc"), [P, S], BF16))
        ss = ps.enter_context(nc.sbuf_tensor(_uq("ss"), [P, S], BF16))
        p_x1 = ps.enter_context(nc.psum_tensor(_uq("px1"), [P, TT], F32))
        p_x2 = ps.enter_context(nc.psum_tensor(_uq("px2"), [P, TT], F32))

        def cload(g):
            kb.dma("dw", g.dma_start(out=cc[:], in_=ctx["dram"]["c_cc"][:, :]))
            kb.dma("dw", g.dma_start(out=ss[:], in_=ctx["dram"]["c_ss"][:, :]))
        kb.pool(cload)
        cc_tok = kb.tok("dw")

        def gt_of(ti, h):
            return base["gt"] + ti * NH + h + 1

        def rp_q(ti, h, k):
            return base["rp"] + (ti * (NH + 1) + h) * 3 + k

        def rp_k(ti, k):
            return base["rp"] + (ti * (NH + 1) + NH) * 3 + k

        def sec_pe(t):
            kb.wait(t, "pe", *ctx.get("psum_free", []))
            kb.wait(t, "pe", ctx.get("lat_tok"), ctx.get("kx_tok"))
            for ti in range(NT2):
                for h in range(NH):
                    bb = qr_base(h)
                    if (ti, h) != (0, 0):
                        ph = h - 1 if h else NH - 1
                        pt = ti if h else ti - 1
                        kb.wait(t, "pe", ("rp", rp_q(pt, ph, 2)))
                    for kc in range(2):
                        t.matmul(p_x1[bb:bb + 32, :],
                                 w12[:, kc * 512 + h * 32:kc * 512 + (h + 1) * 32],
                                 cq[:, kc * S + ti * TT:kc * S + (ti + 1) * TT],
                                 start=(kc == 0), stop=(kc == 1),
                                 tile_position=(0, bb))
                    for kc in range(2):
                        ins = t.matmul(p_x2[bb:bb + 32, :],
                                       w21[:, kc * 512 + h * 32:kc * 512 + (h + 1) * 32],
                                       cq[:, kc * S + ti * TT:kc * S + (ti + 1) * TT],
                                       start=(kc == 0), stop=(kc == 1),
                                       tile_position=(0, bb))
                    kb.inc("gt", ins)
        kb.pe(sec_pe)

        def sec_dve(v):
            kb.wait(v, "dve", cc_tok)
            for ti in range(NT2):
                for h in range(NH):
                    bb = qr_base(h)
                    kb.wait(v, "dve", ("gt", gt_of(ti, h)))
                    tsl = ti * TT
                    ins = v.tensor_tensor(arf1[bb:bb + 32, 0:TT],
                                          p_x1[bb:bb + 32, :],
                                          cc[bb:bb + 32, tsl:tsl + TT], ALU.mult)
                    kb.inc("rp", ins)
                    kb.wait(v, "dve", ("rp", rp_q(ti, h, 1)))
                    ins = v.tensor_tensor(arf1[bb:bb + 32, TT:2 * TT],
                                          p_x2[bb:bb + 32, :],
                                          ss[bb:bb + 32, tsl:tsl + TT], ALU.mult)
                    kb.inc("rp", ins)
                    kb.wait(v, "dve", ("rp", rp_q(ti, h, 2)))
                    ins = v.tensor_tensor(
                        qr_sb[bb:bb + 32, (h // 4) * S + tsl:(h // 4) * S + tsl + TT],
                        arf1[bb:bb + 32, 0:TT], arf1[bb:bb + 32, TT:2 * TT], ALU.add)
                    kb.inc("rp", ins)
                kb.wait(v, "dve", ctx.get("kx_tok"))
                tsl = ti * TT
                ins = v.tensor_tensor(arf1[0:32, 0:TT], kx12[0:32, tsl:tsl + TT],
                                      cc[0:32, tsl:tsl + TT], ALU.mult)
                kb.inc("rp", ins)
                kb.wait(v, "dve", ("rp", rp_k(ti, 1)))
                ins = v.tensor_tensor(arf1[0:32, TT:2 * TT], kx21[0:32, tsl:tsl + TT],
                                      ss[0:32, tsl:tsl + TT], ALU.mult)
                kb.inc("rp", ins)
                kb.wait(v, "dve", ("rp", rp_k(ti, 2)))
                ins = v.tensor_tensor(kr4[0:32, tsl:tsl + TT], arf1[0:32, 0:TT],
                                      arf1[0:32, TT:2 * TT], ALU.add)
                kb.inc("rp", ins)
        kb.dve(sec_dve)

        def sec_sync(q):
            kb.wait(q, "sync", kb.tok("rp"))
            for b in range(1, 4):
                kb.dma("dx", q.dma_start(out=kr4[32 * b:32 * b + 32, :],
                                         in_=kr4[0:32, :]))
        kb.sync(sec_sync)

    ctx["psum_free"] = [kb.tok("rp")]
    return kb.tok("dx")


def stage_attn_b(ctx):
    kb = ctx["kb"]
    nc = ctx["nc"]
    stage_barrier = ctx["x_tok"]

    with ExitStack() as ses:
        cq = ses.enter_context(nc.sbuf_tensor(_uq("bcq"), [P, 2 * S], BF16))
        kr4 = ses.enter_context(nc.sbuf_tensor(_uq("bkr"), [P, S], BF16))

        with ExitStack() as s3:
            kx12 = s3.enter_context(nc.sbuf_tensor(_uq("bkx1"), [32, S], BF16))
            kx21 = s3.enter_context(nc.sbuf_tensor(_uq("bkx2"), [32, S], BF16))

            with ExitStack() as sw:
                w_dq = sw.enter_context(nc.sbuf_tensor(_uq("bwdq"), [P, NPC * DC], BF16))
                w_k12 = sw.enter_context(nc.sbuf_tensor(_uq("bwk1"), [P, NPC * 32], BF16))
                w_k21 = sw.enter_context(nc.sbuf_tensor(_uq("bwk2"), [P, NPC * 32], BF16))

                def loads(g):
                    load_weight(kb, g, w_dq, ctx["dram"]["ab_dq"], free_tok=stage_barrier)
                    load_weight(kb, g, w_k12, ctx["dram"]["ab_kr12"])
                    load_weight(kb, g, w_k21, ctx["dram"]["ab_kr21"])
                kb.pool(loads)
                wtok = kb.tok("dw")
                proj_phase(ctx, [
                    dict(w=w_dq, wk=NPC, oc=2, dst=cq, bias_off=None, wtok=wtok),
                    dict(w=w_k12, wk=NPC, oc=1, dst=kx12, bias_off=None, wtok=wtok,
                         rows=32),
                    dict(w=w_k21, wk=NPC, oc=1, dst=kx21, bias_off=None, wtok=wtok,
                         rows=32),
                ], ln="plain")
            lat_barrier = kb.tok("prj")
            ctx["kx_tok"] = kb.tok("pev")

            with ExitStack() as sw:
                qr_sb = sw.enter_context(nc.sbuf_tensor(_uq("bqr"), [P, 4 * S], BF16))
                w_q12 = sw.enter_context(nc.sbuf_tensor(_uq("bwq1"), [P, 2 * 512], BF16))
                w_q21 = sw.enter_context(nc.sbuf_tensor(_uq("bwq2"), [P, 2 * 512], BF16))

                def loads1(g):
                    load_weight(kb, g, w_q12, ctx["dram"]["ab_qr12"], free_tok=lat_barrier)
                    load_weight(kb, g, w_q21, ctx["dram"]["ab_qr21"])
                kb.pool(loads1)
                ctx["lat_tok"] = kb.tok("pev")
                rope_phase(ctx, cq, kx12, kx21, qr_sb, kr4, w_q12, w_q21)

                def qr_out(q):
                    kb.wait(q, "sync", kb.tok("rp"))
                    kb.dma("dx", q.dma_start(out=ctx["qr_dram"][:, :], in_=qr_sb[:]))
                kb.sync(qr_out)
                qr_out_tok = kb.tok("dx")
            rope_barrier = [kb.tok("gt"), kb.tok("rp"), qr_out_tok]

        kT = ses.enter_context(nc.sbuf_tensor(_uq("bkT"), [P, NPC * S], BF16))
        vtok = ses.enter_context(nc.sbuf_tensor(_uq("bvt"), [P, NH * NKC * 65], BF16))

        with ExitStack() as s4:
            ckv = s4.enter_context(nc.sbuf_tensor(_uq("bckv"), [P, 2 * S], BF16))
            with ExitStack() as sw:
                w_dkv = sw.enter_context(nc.sbuf_tensor(_uq("bwdkv"), [P, NPC * DC], BF16))

                def loadsk(g):
                    load_weight(kb, g, w_dkv, ctx["dram"]["ab_dkv"],
                                free_tok=rope_barrier)
                kb.pool(loadsk)
                wtokk = kb.tok("dw")
                proj_phase(ctx, [
                    dict(w=w_dkv, wk=NPC, oc=2, dst=ckv, bias_off=None, wtok=wtokk),
                ], ln="plain")
            kv_barrier = kb.tok("prj")

            with ExitStack() as sw:
                w_uk = sw.enter_context(nc.sbuf_tensor(_uq("bwuk"), [P, 2 * D], BF16))
                w_uv = sw.enter_context(nc.sbuf_tensor(_uq("bwuv"), [P, 2 * D], BF16))

                def loads2(g):
                    load_weight(kb, g, w_uk, ctx["dram"]["ab_uk"], free_tok=kv_barrier)
                    load_weight(kb, g, w_uv, ctx["dram"]["ab_uv"])
                kb.pool(loads2)
                wtok2 = kb.tok("dw")
                ctx["lat_tok"] = kb.tok("pev")
                r = lat_phase(ctx, ckv, 2, [
                    dict(w=w_uk, wk=2, oc=NPC, dst=kT, bias_off=None, wtok=wtok2),
                    dict(w=w_uv, wk=2, oc=NPC, dst=None, bias_off=None, wtok=wtok2),
                ], vspec=dict(proj_idx=1, vtok=vtok))
                ones_tok = vtok_ones_init(ctx, vtok, after=stage_barrier)
                kT_tok = r["pev"]
                vt_tok = [r["vte"], ones_tok]
        lat_barrier2 = kb.tok("prj")

        with ExitStack() as sw:
            w_uq = sw.enter_context(nc.sbuf_tensor(_uq("bwuq"), [P, 2 * D], BF16))
            w_o = sw.enter_context(nc.sbuf_tensor(_uq("bwo"), [P, NPC * D], BF16))
            qrq = sw.enter_context(nc.sbuf_tensor(_uq("bqrq"), [P, 2 * 4 * QT], BF16))

            def loads3(g):
                load_weight(kb, g, w_uq, ctx["dram"]["ab_uq"], free_tok=lat_barrier2)
                load_weight(kb, g, w_o, ctx["dram"]["ab_o"])
            kb.pool(loads3)
            wtok3 = kb.tok("dw")

            attention_phase(ctx, dict(
                kind="b", scale=float(1.0 / np.sqrt(np.float32(96.0))),
                consumer="resid",
                kT=kT, kT_tok=kT_tok, vtok=vtok, vtok_tok=vt_tok,
                qlat=cq, qlat_tok=kb.tok("pev"),
                qrq=qrq, qr_tok=qr_out_tok, kr4=kr4,
                wq=w_uq, wq_tok=wtok3, wout=w_o, wout_tok=wtok3,
            ))
    ctx["x_tok"] = kb.tok("cons")



def kernel(**inputs):
    in_maps = prep_inputs(inputs)
    nc, es = build_core()
    try:
        res = run_bass_kernel_spmd(nc, in_maps, core_ids=list(range(8)))
    finally:
        es.close()
    y = np.concatenate([np.asarray(r["y_fm"]) for r in res.results], axis=0)
    return np.ascontiguousarray(y.transpose(0, 2, 1)).astype(np.float32)
